# version 16 (final)
# baseline (speedup 1.0000x reference)
import numpy as np
import ml_dtypes
from contextlib import ExitStack

import concourse.bass as bass
import concourse.mybir as mybir
from concourse.bass_utils import run_bass_kernel_spmd
from concourse.alu_op_type import AluOpType as Op

BF = mybir.dt.bfloat16
F32 = mybir.dt.float32
AF = mybir.ActivationFunctionType
AX = mybir.AxisListType

C_THR = 0.5009765625
B, T, D, H, DH = 4, 4096, 2048, 16, 128
TS = 512
NL = 4
TOK = NL * TS
NCH = D // 128
HF = 8192
EPS_LN = 1e-5
EPS_RMS = 1.1920929e-7
INV64 = 1.0 / 64.0
ENGINES = ("sync", "vector", "scalar", "tensor", "gpsimd")

NPBF = ml_dtypes.bfloat16


class Prog:

    def __init__(self, nc, stack):
        self.nc = nc
        self.stack = stack
        self.cur = None
        self.cnt = {}
        self.sems = {}
        self.active_pool_sems = set()

    def sem(self, name):
        if name not in self.sems:
            self.sems[name] = self.stack.enter_context(self.nc.semaphore(name))
        if name not in self.cnt:
            self.cnt[name] = 0
        return self.sems[name]

    AUTOFENCE = {"vector": "DV", "scalar": "AC"}

    def op(self, engname, fn, inc=None, n=1, nofence=False):
        auto = self.AUTOFENCE.get(engname)
        if auto is not None and not nofence:
            self.wait(engname, auto)
            if inc is None:
                inc = auto
        if inc is not None:
            self.sem(inc)
        if self.cur == engname:
            instr = fn(getattr(self.nc, engname))
            if inc is not None:
                instr.then_inc(self.sems[inc], n)
        if inc is not None:
            self.cnt[inc] += n
            return self.cnt[inc]
        return None

    def wait(self, engname, semname, val=None):
        self.sem(semname)
        v = self.cnt[semname] if val is None else val
        if v <= 0:
            return
        if self.cur == engname:
            getattr(self.nc, engname).wait_ge(self.sems[semname], v)

    def waitl(self, engname, items):
        for it in items:
            if it is not None:
                self.wait(engname, it[0], it[1])

    def drain_pools(self, engname):
        for sem in sorted(self.active_pool_sems):
            self.wait(engname, sem, self.cnt[sem])
        self.active_pool_sems = set()


class Pool:

    def __init__(self, P, bufs, semnames):
        self.P = P
        self.slots = [{"buf": b, "sem": s, "ready": 0, "free": None}
                      for b, s in zip(bufs, semnames)]
        for s in self.slots:
            if s["sem"] is not None:
                P.sem(s["sem"])
                P.active_pool_sems.add(s["sem"])
        self.i = 0

    def next(self):
        s = self.slots[self.i % len(self.slots)]
        self.i += 1
        return s

    def load(self, q, src_fn, waitl=()):
        s = self.next()
        if s["free"] is not None:
            self.P.waitl(q, s["free"] if isinstance(s["free"], list) else [s["free"]])
        for w in waitl:
            self.P.wait(q, *w)
        self.P.op(q, lambda e: e.dma_start(out=s["buf"][:], in_=src_fn()),
                  inc=s["sem"], n=16)
        s["ready"] = self.P.cnt[s["sem"]]
        return s


def mid3(ap2, mid_step, mid_count):
    return bass.AP(ap2.tensor, ap2.offset, [ap2.ap[0], [mid_step, mid_count], ap2.ap[1]])


def bview(ap2, bcast):
    return bass.AP(ap2.tensor, ap2.offset, list(ap2.ap) + [[0, bcast]])


def build(taps=(), upto=99):
    nc = bass.Bass("TRN2", target_bir_lowering=False)

    def dram(name, shape, dt, kind=None):
        if name in taps:
            kind = "ExternalOutput"
        if kind is None:
            return nc.dram_tensor(name, shape, dt).ap()
        return nc.dram_tensor(name, shape, dt, kind=kind).ap()

    x_own = dram("x_own", [TOK, D], F32, "ExternalInput")
    wq_in = dram("wq", [D, D], F32, "ExternalInput")
    wk_in = dram("wk", [D, D], F32, "ExternalInput")
    wv_in = dram("wv", [D, D], F32, "ExternalInput")
    wo_in = dram("wo", [D, D], F32, "ExternalInput")
    wgu_my = dram("wgu_my", [4 * 512, D], F32, "ExternalInput")
    wp_my = dram("wp_my", [2 * 128, HF], F32, "ExternalInput")
    c_mask = dram("c_mask", [128, NL * 8 * TS], BF, "ExternalInput")
    c_ones_bf = dram("c_ones_bf", [128, 128], BF, "ExternalInput")
    c_ones_f = dram("c_ones_f", [1, 128], F32, "ExternalInput")
    c_id_bf = dram("c_id_bf", [128, 128], BF, "ExternalInput")
    c_id_f = dram("c_id_f", [128, 128], F32, "ExternalInput")
    out = dram("out", [TOK, D], F32, "ExternalOutput")

    wbf_scr = dram("wbf_scr", [4 * 128, NCH * D], BF)
    contrib_w = dram("contrib_w", [6 * 128, 16 * TS], BF)
    gath_w = nc.dram_tensor("gath_w", [8 * 6 * 128, 16 * TS], BF,
                            addr_space="Shared").ap()

    def wgu_row(g):
        return ((g % 8) * 6 + g // 8) * 128

    def wp_row(m):
        return ((m % 8) * 6 + 4 + m // 8) * 128
    xoT_sw = dram("xoT_sw", [NL * 128, 16 * TS], BF)
    xoTf_sw = dram("xoTf_sw", [NL * 16 * 128, TS], F32)
    kc_scr = dram("kc_scr", [D, TOK], BF)
    vc_scr = dram("vc_scr", [H * 128, TOK], BF)
    kg_scr = dram("kg_scr", [2 * D, TOK], BF)
    vg_scr = dram("vg_scr", [2 * H * 128, TOK], BF)
    ctx_sw = dram("ctx_sw", [NL * 128, H * TS], BF)
    x1f_sw = dram("x1f_sw", [NL * 16 * 128, TS], F32)
    x1b_sw = dram("x1b_sw", [NL * 128, 16 * TS], BF)

    gstack = ExitStack()
    with gstack:
        P = Prog(nc, gstack)
        for s in ("IN", "SC", "PE", "DV", "AC"):
            P.sem(s)

        def sb(st, name, shape, dt, side="left"):
            return st.enter_context(nc.sbuf_tensor(name, shape, dt, side=side))

        def ps(st, name, dt=F32):
            cols = TS if dt == F32 else 1024
            return st.enter_context(nc.psum_tensor(name, [128, cols], dt))

        def run_phase(allocf, emitf):
            with ExitStack() as st:
                bufs = allocf(st)
                with nc.Block() as blk:
                    snap = dict(P.cnt)
                    for en in ENGINES:
                        P.cnt = dict(snap)
                        P.cur = en
                        P.active_pool_sems = set()

                        def body(eng, _f=emitf, _b=bufs):
                            _f(P, _b)
                            P.drain_pools("gpsimd")
                        getattr(blk, en)(body)
                    P.cur = None

        ones_bf = sb(gstack, "ones_bf", [128, 128], BF, side="right")
        ones_f = sb(gstack, "ones_f", [1, 128], F32, side="right")
        id_bf = sb(gstack, "id_bf", [128, 128], BF, side="right")
        id_f = sb(gstack, "id_f", [128, 128], F32, side="right")

        def ph_consts(P, A):
            for dst, src in ((ones_bf, c_ones_bf), (ones_f, c_ones_f),
                             (id_bf, c_id_bf), (id_f, c_id_f)):
                P.op("sync", lambda e, d=dst, s=src: e.dma_start(out=d[:], in_=s),
                     inc="IN", n=16)
            P.wait("sync", "IN")

        run_phase(lambda st: {}, ph_consts)

        def q_alloc(st):
            return {
                "wfin": [sb(st, f"q_wf{i}", [128, D], F32) for i in range(2)],
                "wcst": [sb(st, f"q_wc{i}", [128, D], BF) for i in range(2)],
                "win": [sb(st, f"q_win{i}", [128, 4 * D], F32) for i in range(2)],
                "wb": sb(st, "q_wb", [128, D], BF),
                "t1": sb(st, "q_t1", [128, D], BF),
                "t2": sb(st, "q_t2", [128, D], BF),
                "sums": sb(st, "q_sums", [128, D // 64], F32),
                "sbf": sb(st, "q_sbf", [128, D // 64], BF),
                "thr": sb(st, "q_thr", [128, D // 64], F32),
                "wqt": [sb(st, f"q_wqt{i}", [128, D], BF) for i in range(2)],
                "stage": [sb(st, f"q_stg{i}", [128, 16 * TS], BF) for i in range(2)],
                "psq": [ps(st, f"q_ps{i}", BF) for i in range(2)],
            }

        def q_emit(P, A):
            win = Pool(P, A["win"], ["qa0", "qa1"])
            stage = Pool(P, A["stage"], ["w0", "w1"])
            wfin = Pool(P, A["wfin"], ["qd0", "qd1"])
            wcst = Pool(P, A["wcst"], ["w6", "w7"])
            wjobs = [(wi, c) for wi in range(4) for c in range(NCH)]
            wstate = [0]

            def pump_w(n):
                for _ in range(n):
                    if wstate[0] >= len(wjobs):
                        return
                    wi, c = wjobs[wstate[0]]
                    wstate[0] += 1
                    wt = (wk_in, wv_in, wq_in, wo_in)[wi]
                    s = wfin.load("sync", lambda _c=c, _w=wt: _w[_c * 128:(_c + 1) * 128, :])
                    dstc = wcst.next()
                    if dstc["free"] is not None:
                        P.wait("scalar", *dstc["free"])
                    P.wait("scalar", s["sem"], s["ready"])
                    acw = P.op("scalar", lambda e, _s=s, _d=dstc: e.copy(
                        out=_d["buf"][:], in_=_s["buf"][:]), inc="AC", nofence=True)
                    s["free"] = ("AC", acw)
                    P.wait("gpsimd", "AC", acw)
                    P.op("gpsimd", lambda e, _wi=wi, _c=c, _d=dstc: e.dma_start(
                        out=wbf_scr[_wi * 128:(_wi + 1) * 128, _c * D:(_c + 1) * D],
                        in_=_d["buf"][:]), inc=dstc["sem"], n=16)
                    dstc["free"] = (dstc["sem"], P.cnt[dstc["sem"]])
            psq_free = [None, None]
            wqt_free = [None, None]
            groups = [("wgu", g) for g in range(4)] + [("wp", m) for m in range(2)]
            for gi, (kind, g) in enumerate(groups):
                if kind == "wgu":
                    src = lambda _k=g: bass.AP(wgu_my.tensor, _k * 4 * 128 * D,
                                               [[D, 128], [128 * D, 4], [1, D]])
                else:
                    src = lambda _k=g: bass.AP(wp_my.tensor, _k * 128 * HF,
                                               [[HF, 128], [D, 4], [1, D]])
                ws = win.load("sync", src)
                pump_w(11)
                stg = stage.next()
                if stg["free"] is not None:
                    P.wait("vector", *stg["free"])
                for sub in range(4):
                    P.wait("vector", ws["sem"], ws["ready"])
                    wsub = lambda _s=sub, _b=ws["buf"]: _b[:, _s * D:(_s + 1) * D]
                    P.op("vector", lambda e: e.tensor_copy(out=A["wb"][:], in_=wsub()))
                    P.op("vector", lambda e: e.tensor_reduce(
                        out=A["sums"][:],
                        in_=A["wb"][:].rearrange("p (g k) -> p g k", k=64),
                        axis=AX.X, op=Op.add, apply_absolute_value=True))
                    P.op("vector", lambda e: e.tensor_scalar(
                        out=A["sbf"][:], in0=A["sums"][:], scalar1=INV64,
                        scalar2=1e-8, op0=Op.mult, op1=Op.max))
                    P.op("vector", lambda e: e.tensor_scalar(
                        out=A["thr"][:], in0=A["sbf"][:], scalar1=C_THR, scalar2=0.0, op0=Op.mult, op1=Op.add))
                    thr_b = bview(A["thr"][:], 64)
                    sbf_b = bview(A["sbf"][:], 64)
                    w3 = A["wb"][:].rearrange("p (g k) -> p g k", k=64)
                    t13 = A["t1"][:].rearrange("p (g k) -> p g k", k=64)
                    t23 = A["t2"][:].rearrange("p (g k) -> p g k", k=64)
                    wi = (gi * 4 + sub) % 2
                    wqt = A["wqt"][wi]
                    if wqt_free[wi] is not None:
                        P.wait("vector", *wqt_free[wi])
                    P.op("vector", lambda e: e.tensor_tensor(
                        out=t13, in0=w3, in1=thr_b, op=Op.is_gt))
                    P.op("vector", lambda e: e.scalar_tensor_tensor(
                        out=t23, in0=w3, scalar=-1.0, in1=thr_b,
                        op0=Op.mult, op1=Op.is_gt))
                    P.op("vector", lambda e: e.tensor_tensor(
                        out=A["t1"][:], in0=A["t1"][:], in1=A["t2"][:], op=Op.subtract))
                    P.op("vector", lambda e: e.tensor_tensor(
                        out=t13, in0=t13, in1=sbf_b, op=Op.mult))
                    P.op("vector", lambda e: e.tensor_tensor(
                        out=A["t1"][:], in0=A["t1"][:], in1=A["wb"][:], op=Op.subtract))
                    dvv = P.op("vector", lambda e, _w=wqt: e.tensor_tensor(
                        out=_w[:], in0=A["wb"][:], in1=A["t1"][:], op=Op.add), inc="DV")
                    for k4 in range(4):
                        pi = k4 % 2
                        if psq_free[pi] is not None:
                            P.wait("tensor", *psq_free[pi])
                        P.wait("tensor", "DV", dvv)
                        for kk in range(4):
                            k = k4 * 4 + kk
                            pe = P.op("tensor", lambda e, _k=k, _pi=pi, _kk=kk, _w=wqt: e.transpose(
                                out=A["psq"][_pi][:, _kk * 128:(_kk + 1) * 128],
                                in_=_w[:, _k * 128:(_k + 1) * 128],
                                identity=id_bf[:]), inc="PE")
                        P.wait("vector", "PE", pe)
                        if kind == "wgu":
                            dst_ap = bass.AP(stg["buf"][:].tensor,
                                             stg["buf"][:].offset + (4 * k4 * TS + sub * 128),
                                             [stg["buf"][:].ap[0], [TS, 4], [1, 128]])
                            src_ap = A["psq"][pi][:, 0:TS].rearrange(
                                "p (a b) -> p a b", b=128)
                        else:
                            dst_ap = stg["buf"][:, sub * 2048 + k4 * TS:
                                                sub * 2048 + (k4 + 1) * TS]
                            src_ap = A["psq"][pi][:, 0:TS]
                        dve = P.op("vector", lambda e, _d=dst_ap, _s=src_ap: e.tensor_copy(
                            out=_d, in_=_s), inc="DV")
                        psq_free[pi] = ("DV", dve)
                ws["free"] = ("DV", P.cnt["DV"])
                wqt_free[wi] = ("PE", P.cnt["PE"])
                P.wait("gpsimd", "DV")
                dst = lambda _gi=gi: contrib_w[_gi * 128:(_gi + 1) * 128, :]
                P.op("gpsimd", lambda e, _sg=stg: e.dma_start(out=dst(), in_=_sg["buf"][:]),
                     inc=stg["sem"], n=16)
                stg["free"] = (stg["sem"], P.cnt[stg["sem"]])
            pump_w(64)

        if upto >= 1:
            run_phase(q_alloc, q_emit)

        def xt_alloc(st):
            return {
                "xin": [sb(st, f"x_in{i}", [128, 4 * D], F32) for i in range(2)],
                "xb": sb(st, "x_xb", [128, 4 * D], BF),
                "stg_b": [sb(st, f"x_sb{i}", [128, 16 * TS], BF) for i in range(2)],
                "stg_f": [sb(st, "x_sf0", [128, 16 * TS], F32)],
                "psb": [ps(st, f"x_pb{i}", BF) for i in range(2)],
                "psf": [ps(st, f"x_pf{i}", F32) for i in range(2)],
            }

        def xt_emit(P, A):
            P.wait("sync", "SC", P.cnt["SC"])
            xin = Pool(P, A["xin"], ["qa0", "qa1"])
            stgb_p = Pool(P, A["stg_b"], ["w0", "w1"])
            stgf_p = Pool(P, A["stg_f"], ["w2"])
            psb_free = [None, None]
            psf_free = [None, None]
            jobs = [("own", s) for s in range(NL)]
            for kind, sec in jobs:
                src_t = x_own.tensor
                xs = xin.load("sync",
                              lambda _s=sec, _t=src_t: bass.AP(
                                  _t, _s * TS * D, [[D, 128], [128 * D, 4], [1, D]]))
                P.wait("vector", xs["sem"], xs["ready"])
                dvc = P.op("vector", lambda e, _x=xs: e.tensor_copy(
                    out=A["xb"][:], in_=_x["buf"][:]), inc="DV")
                stgb = stgb_p.next()
                if stgb["free"] is not None:
                    P.wait("vector", *stgb["free"])
                stgf = None
                if kind == "own":
                    stgf = stgf_p.next()
                    if stgf["free"] is not None:
                        P.wait("vector", *stgf["free"])
                for dc in range(NCH):
                    pi = dc % 2
                    if psb_free[pi] is not None:
                        P.wait("tensor", *psb_free[pi])
                    P.wait("tensor", "DV", dvc)
                    for tile in range(4):
                        pe = P.op("tensor", lambda e, _c=tile * D + dc * 128, _p=pi,
                                  _t=tile: e.transpose(
                            out=A["psb"][_p][:, _t * 128:(_t + 1) * 128],
                            in_=A["xb"][:, _c:_c + 128],
                            identity=id_bf[:]), inc="PE")
                    P.wait("vector", "PE", pe)
                    dve = P.op("vector", lambda e, _p=pi, _d=dc * TS, _sg=stgb: e.tensor_copy(
                        out=_sg["buf"][:, _d:_d + TS], in_=A["psb"][_p][:, 0:TS]),
                        inc="DV")
                    psb_free[pi] = ("DV", dve)
                    if kind == "own":
                        if psf_free[pi] is not None:
                            P.wait("tensor", *psf_free[pi])
                        P.wait("tensor", xs["sem"], xs["ready"])
                        for tile in range(4):
                            pe2 = P.op("tensor", lambda e, _c=tile * D + dc * 128, _p=pi,
                                       _t=tile, _x=xs: e.transpose(
                                out=A["psf"][_p][:, _t * 128:(_t + 1) * 128],
                                in_=_x["buf"][:, _c:_c + 128],
                                identity=id_f[:]), inc="PE")
                        P.wait("vector", "PE", pe2)
                        dve2 = P.op("vector", lambda e, _p=pi, _d=dc * TS, _sg=stgf: e.tensor_copy(
                            out=_sg["buf"][:, _d:_d + TS], in_=A["psf"][_p][:, 0:TS]),
                            inc="DV")
                        psf_free[pi] = ("DV", dve2)
                xs["free"] = [("DV", P.cnt["DV"]), ("PE", P.cnt["PE"])]
                P.wait("gpsimd", "DV")
                dstp = lambda _s=sec: xoT_sw[_s * 128:(_s + 1) * 128, :]
                P.op("gpsimd", lambda e, _sg=stgb: e.dma_start(out=dstp(), in_=_sg["buf"][:]),
                     inc=stgb["sem"], n=16)
                stgb["free"] = (stgb["sem"], P.cnt[stgb["sem"]])
                if kind == "own":
                    for dc in range(NCH):
                        row0 = (sec * 16 + dc) * 128
                        P.op("gpsimd", lambda e, _r=row0, _d=dc, _sg=stgf: e.dma_start(
                            out=xoTf_sw[_r:_r + 128, :],
                            in_=_sg["buf"][:, _d * TS:(_d + 1) * TS]),
                            inc=stgf["sem"], n=16)
                    stgf["free"] = (stgf["sem"], P.cnt[stgf["sem"]])

        if upto >= 2:
            run_phase(xt_alloc, xt_emit)

        with ExitStack() as midst:
            qT = sb(midst, "qT", [128, NCH * TOK], BF, side="right")

            def kvq_alloc(st):
                return {
                    "wbuf": sb(st, "kv_wbuf", [128, NCH * D], BF),
                    "xsec": [sb(st, f"kv_xs{i}", [128, 16 * TS], BF) for i in range(2)],
                    "ostg": [sb(st, f"kv_os{i}", [128, TS], BF) for i in range(4)],
                    "vstg": [sb(st, "kv_vs0", [128, 4 * D], BF)],
                    "pk": [ps(st, f"kv_pk{i}") for i in range(2)],
                }

            def kvq_emit(P, A):
                P.wait("sync", "SC", P.cnt["SC"])
                xsec = Pool(P, A["xsec"], ["qb0", "qb1"])
                ostg = Pool(P, A["ostg"], ["w2", "w3", "w4", "w5"])
                vstg = Pool(P, A["vstg"], ["w6"])
                pk_free = [None, None]
                wsem = ["qa0", "qa1", "qd0"]
                wseq = [0]

                def load_w(wi):
                    sem = wsem[wseq[0] % 3]
                    wseq[0] += 1
                    P.sem(sem)
                    P.active_pool_sems.add(sem)
                    P.wait("sync", "PE", P.cnt["PE"])
                    P.op("sync", lambda e, _wi=wi: e.dma_start(
                        out=A["wbuf"][:], in_=wbf_scr[_wi * 128:(_wi + 1) * 128, :]),
                        inc=sem, n=16)
                    P.wait("tensor", sem, P.cnt[sem])
                    return 0

                def proj_pass(src_plane_fn, nsec, evac_fn):
                    for sec in range(nsec):
                        xs = xsec.load("sync", lambda _s=sec: src_plane_fn(_s))
                        for c in range(NCH):
                            pi = c % 2
                            if pk_free[pi] is not None:
                                P.wait("tensor", *pk_free[pi])
                            P.wait("tensor", xs["sem"], xs["ready"])
                            for dc in range(NCH):
                                P.op("tensor", lambda e, _dc=dc, _c=c, _pi=pi, _x=xs: e.matmul(
                                    out=A["pk"][_pi][:],
                                    lhsT=A["wbuf"][:, _dc * D + _c * 128:_dc * D + _c * 128 + 128],
                                    rhs=_x["buf"][:, _dc * TS:(_dc + 1) * TS],
                                    start=(_dc == 0), stop=(_dc == NCH - 1)),
                                    inc="PE" if dc == NCH - 1 else None)
                            pk_free[pi] = evac_fn(sec, c, pi)
                        xs["free"] = ("PE", P.cnt["PE"])

                load_w(0)

                def k_evac(sec, c, pi):
                    pev = P.cnt["PE"]
                    og = ostg.next()
                    if og["free"] is not None:
                        P.wait("vector", *og["free"])
                    P.wait("vector", "PE", pev)
                    dve = P.op("vector", lambda e, _pi=pi, _o=og: e.tensor_copy(
                        out=_o["buf"][:], in_=A["pk"][_pi][:]), inc="DV")
                    P.wait("gpsimd", "DV", dve)
                    P.op("gpsimd", lambda e, _c=c, _s=sec, _o=og: e.dma_start(
                        out=kc_scr[_c * 128:(_c + 1) * 128, _s * TS:(_s + 1) * TS],
                        in_=_o["buf"][:]), inc=og["sem"], n=16)
                    og["free"] = (og["sem"], P.cnt[og["sem"]])
                    return ("DV", dve)

                proj_pass(lambda s: xoT_sw[s * 128:(s + 1) * 128, :], NL, k_evac)
                for _s in ("w2", "w3", "w4", "w5"):
                    P.wait("gpsimd", _s, P.cnt.get(_s, 0))
                P.op("gpsimd", lambda e: e.collective_compute(
                    "AllGather", mybir.AluOpType.bypass,
                    replica_groups=[[0, 1], [2, 3], [4, 5], [6, 7]],
                    ins=[kc_scr], outs=[kg_scr]), inc="CC", n=1)

                load_w(1)
                w_rdy = P.cnt["DV"]
                for sec in range(NL):
                    xs = xsec.load("sync",
                                   lambda _s=sec: xoT_sw[_s * 128:(_s + 1) * 128, :])
                    vg = vstg.next()
                    if vg["free"] is not None:
                        P.wait("vector", *vg["free"])
                    for stp in range(4):
                        for cs in range(4):
                            pi = (stp * 4 + cs) % 2
                            if pk_free[pi] is not None:
                                P.wait("tensor", *pk_free[pi])
                            P.wait("tensor", xs["sem"], xs["ready"])
                            for dc in range(NCH):
                                P.op("tensor", lambda e, _dc=dc, _st=stp, _cs=cs, _pi=pi, _x=xs: e.matmul(
                                    out=A["pk"][_pi][:],
                                    lhsT=_x["buf"][:, _dc * TS + _st * 128:_dc * TS + _st * 128 + 128],
                                    rhs=A["wbuf"][:, _dc * D + _cs * TS:_dc * D + (_cs + 1) * TS],
                                    start=(_dc == 0), stop=(_dc == NCH - 1)),
                                    inc="PE" if dc == NCH - 1 else None)
                            pev = P.cnt["PE"]
                            P.wait("vector", "PE", pev)
                            dve = P.op("vector", lambda e, _pi=pi, _st=stp, _cs=cs, _v=vg: e.tensor_copy(
                                out=_v["buf"][:, _st * D + _cs * TS:_st * D + (_cs + 1) * TS],
                                in_=A["pk"][_pi][:]), inc="DV")
                            pk_free[pi] = ("DV", dve)
                    P.wait("gpsimd", "DV")
                    for h in range(H):
                        base = vg["buf"][:, h * 128:h * 128 + 128]
                        P.op("gpsimd", lambda e, _h=h, _s=sec, _b=base: e.dma_start(
                            out=vc_scr[_h * 128:(_h + 1) * 128, _s * TS:(_s + 1) * TS],
                            in_=mid3(_b, D, 4)),
                            inc=vg["sem"], n=16)
                    vg["free"] = (vg["sem"], P.cnt[vg["sem"]])
                    xs["free"] = ("PE", P.cnt["PE"])
                P.wait("gpsimd", "w6", P.cnt.get("w6", 0))
                P.op("gpsimd", lambda e: e.collective_compute(
                    "AllGather", mybir.AluOpType.bypass,
                    replica_groups=[[0, 1], [2, 3], [4, 5], [6, 7]],
                    ins=[vc_scr], outs=[vg_scr]), inc="CC", n=1)

                load_w(2)

                def q_evac(sec, c, pi):
                    pev = P.cnt["PE"]
                    P.wait("vector", "PE", pev)
                    dve = P.op("vector", lambda e, _pi=pi, _c=c, _s=sec: e.tensor_copy(
                        out=qT[:, _c * TOK + _s * TS:_c * TOK + (_s + 1) * TS],
                        in_=A["pk"][_pi][:]), inc="DV")
                    return ("DV", dve)

                proj_pass(lambda s: xoT_sw[s * 128:(s + 1) * 128, :], NL, q_evac)

            if upto >= 3:
                run_phase(kvq_alloc, kvq_emit)

            def at_alloc(st):
                return {
                    "kh": [sb(st, f"at_k{i}", [128, T], BF) for i in range(2)],
                    "vh": [sb(st, f"at_v{i}", [128, T], BF) for i in range(2)],
                    "att": [sb(st, f"at_a{i}", [128, TS], BF) for i in range(4)],
                    "mtmp": [sb(st, f"at_m{i}", [128, TS], F32) for i in range(2)],
                    "cs": sb(st, "at_cs", [128, 32], F32),
                    "suff": sb(st, "at_sf", [128, NL], F32),
                    "zf": sb(st, "at_zf", [1, TS], F32),
                    "zr": sb(st, "at_zr", [1, TS], F32),
                    "zb": sb(st, "at_zb", [128, TS], F32),
                    "cstg": [sb(st, f"at_cg{i}", [128, TS], BF) for i in range(8)],
                    "psc": [ps(st, f"at_sc{i}") for i in range(4)],
                    "pz": [ps(st, "at_z0")],
                    "pctx": [ps(st, f"at_cx{i}") for i in range(2)],
                    "pcs": ps(st, "at_pcs"),
                    "mask": sb(st, "mask_sb", [128, NL * 8 * TS], BF),
                }

            def at_emit(P, A):
                P.wait("sync", "SC", P.cnt["SC"])
                kh_p = Pool(P, A["kh"], ["qa0", "qa1"])
                vh_p = Pool(P, A["vh"], ["qb0", "qb1"])
                cstg = Pool(P, A["cstg"],
                            ["w0", "w1", "w2", "w3", "w4", "w5", "w6", "w7"])
                for _s in ("w0", "w1"):
                    P.wait("gpsimd", _s, P.cnt.get(_s, 0))
                P.op("gpsimd", lambda e: e.collective_compute(
                    "AllGather", mybir.AluOpType.bypass,
                    replica_groups=[list(range(8))],
                    ins=[contrib_w], outs=[gath_w]), inc="CC", n=1)
                psc_free = [None, None, None, None]
                pz_free = [None]
                pctx_free = [None, None]
                pcs_free = [None]
                mtmp_free = [None, None]
                mask_sb = A["mask"]
                P.op("sync", lambda e: e.dma_start(out=mask_sb[:], in_=c_mask),
                     inc="IN", n=16)
                mask_rdy = P.cnt["IN"]
                P.wait("sync", "IN", mask_rdy)
                for h in range(H):
                    kh = kh_p.next()
                    if kh["free"] is not None:
                        P.waitl("sync", [kh["free"]])
                    P.wait("sync", "CC", 1)
                    for E in range(8):
                        P.op("sync", lambda e, _h=h, _E=E, _k=kh: e.dma_start(
                            out=_k["buf"][:, _E * TS:(_E + 1) * TS],
                            in_=kg_scr[(_E % 2) * D + _h * 128:(_E % 2) * D + _h * 128 + 128,
                                       (_E // 2) * TS:(_E // 2 + 1) * TS]),
                            inc=kh["sem"], n=16)
                    kh["ready"] = P.cnt[kh["sem"]]
                    vh = vh_p.next()
                    if vh["free"] is not None:
                        P.waitl("sync", [vh["free"]])
                    P.wait("sync", "CC", 2)
                    for E in range(8):
                        P.op("sync", lambda e, _h=h, _E=E, _v=vh: e.dma_start(
                            out=_v["buf"][:, _E * 4 * 128:(_E + 1) * 4 * 128],
                            in_=vg_scr[(_E % 2) * D + _h * 128:(_E % 2) * D + _h * 128 + 128,
                                       (_E // 2) * TS:(_E // 2 + 1) * TS]),
                            inc=vh["sem"], n=16)
                    vh["ready"] = P.cnt[vh["sem"]]
                    if pcs_free[0] is not None:
                        P.wait("tensor", *pcs_free[0])
                    P.wait("tensor", vh["sem"], vh["ready"])
                    for j in range(32):
                        P.op("tensor", lambda e, _j=j, _v=vh: e.matmul(
                            out=A["pcs"][:, _j:_j + 1],
                            lhsT=_v["buf"][:, _j * 128:(_j + 1) * 128],
                            rhs=ones_bf[:, 0:1], start=True, stop=True),
                            inc="PE" if j == 31 else None)
                    pev = P.cnt["PE"]
                    P.wait("vector", "PE", pev)
                    dve = P.op("vector", lambda e: e.tensor_copy(
                        out=A["cs"][:], in_=A["pcs"][:, 0:32]), inc="DV")
                    pcs_free[0] = ("DV", dve)
                    for l in range(3):
                        P.op("vector", lambda e, _l=l: e.tensor_reduce(
                            out=A["suff"][:, _l:_l + 1], in_=A["cs"][:, 8 * (_l + 1):32],
                            axis=AX.X, op=Op.add))
                    P.op("vector", lambda e: e.memset(A["suff"][:, 3:4], 0.0), inc="DV")
                    suff_rdy = P.cnt["DV"]
                    jobs = [(l, j) for l in range(NL) for j in range(8 * l + 8)]

                    if True:
                        def emit_score(gi, _h=h, _kh=kh):
                            _l, j = jobs[gi]
                            si = gi % 4
                            rel = j - 8 * _l
                            if psc_free[si] is not None:
                                P.wait("tensor", *psc_free[si])
                            P.wait("tensor", kh["sem"], kh["ready"])
                            pesc = P.op("tensor", lambda e, _j=j, _si=si: e.matmul(
                                out=A["psc"][_si][:],
                                lhsT=_kh["buf"][:, _j * 128:(_j + 1) * 128],
                                rhs=qT[:, _h * TOK + _l * TS:_h * TOK + (_l + 1) * TS],
                                start=True, stop=True), inc="PE")
                            att = A["att"][gi % 4]
                            if rel >= 0:
                                mi = j % 2
                                P.wait("vector", "PE", pesc)
                                P.wait("vector", "IN", mask_rdy)
                                if mtmp_free[mi] is not None:
                                    P.wait("vector", *mtmp_free[mi])
                                dvm = P.op("vector", lambda e, _si=si, _mi=mi, _rel=rel: e.tensor_tensor(
                                    out=A["mtmp"][_mi][:], in0=A["psc"][_si][:],
                                    in1=mask_sb[:, (_l * 8 + _rel) * TS:(_l * 8 + _rel + 1) * TS],
                                    op=Op.mult), inc="DV")
                                psc_free[si] = ("DV", dvm)
                                P.wait("scalar", "DV", dvm)
                                ac = P.op("scalar", lambda e, _mi=mi, _a=att: e.activation(
                                    out=_a[:], in_=A["mtmp"][_mi][:], func=AF.Exp,
                                    scale=INV64), inc="AC")
                                mtmp_free[mi] = ("AC", ac)
                            else:
                                P.wait("scalar", "PE", pesc)
                                ac = P.op("scalar", lambda e, _si=si, _a=att: e.activation(
                                    out=_a[:], in_=A["psc"][_si][:], func=AF.Exp,
                                    scale=INV64), inc="AC")
                                psc_free[si] = ("AC", ac)
                            return ac, att

                        pend = {}
                        LOOK = 4
                        for gi in range(min(LOOK, len(jobs))):
                            pend[gi] = emit_score(gi)
                        for gi, (l, j) in enumerate(jobs):
                            nblk = 8 * l + 8
                            zi, ci = 0, l % 2
                            ac, att = pend.pop(gi)
                            if j == 0 and pz_free[zi] is not None:
                                P.wait("tensor", *pz_free[zi])
                            if j == 0 and pctx_free[ci] is not None:
                                P.wait("tensor", *pctx_free[ci])
                            P.wait("tensor", "AC", ac)
                            P.op("tensor", lambda e, _zi=zi, _a=att, _j=j, _n=nblk: e.matmul(
                                out=A["pz"][_zi][0:1, :], lhsT=ones_bf[:, 0:1], rhs=_a[:],
                                start=(_j == 0), stop=(_j == _n - 1)))
                            P.op("tensor", lambda e, _ci=ci, _a=att, _j=j, _n=nblk, _v=vh: e.matmul(
                                out=A["pctx"][_ci][:],
                                lhsT=_v["buf"][:, _j * 128:(_j + 1) * 128], rhs=_a[:],
                                start=(_j == 0), stop=(_j == _n - 1)),
                                inc="PE")
                            if gi + LOOK < len(jobs):
                                pend[gi + LOOK] = emit_score(gi + LOOK)
                            if j != nblk - 1:
                                continue
                            pe_pv = P.cnt["PE"]
                            count = float(T - nblk * 128)
                            P.wait("vector", "PE", pe_pv)
                            P.op("vector", lambda e, _zi=zi, _c=count: e.tensor_scalar(
                                out=A["zf"][:], in0=A["pz"][_zi][0:1, :], scalar1=_c,
                                scalar2=0.0, op0=Op.add, op1=Op.add))
                            dvz = P.op("vector", lambda e: e.reciprocal(
                                out=A["zr"][:], in_=A["zf"][:]), inc="DV")
                            pz_free[zi] = ("DV", dvz)
                            if pcs_free[0] is not None:
                                P.wait("tensor", *pcs_free[0])
                            P.wait("tensor", "DV", dvz)
                            pezb = P.op("tensor", lambda e: e.matmul(
                                out=A["pcs"][:], lhsT=ones_f[:], rhs=A["zr"][:],
                                start=True, stop=True), inc="PE")
                            P.wait("vector", "PE", pezb)
                            dvzb = P.op("vector", lambda e: e.tensor_copy(
                                out=A["zb"][:], in_=A["pcs"][:]), inc="DV")
                            pcs_free[0] = ("DV", dvzb)
                            cg = cstg.next()
                            if cg["free"] is not None:
                                P.wait("vector", *cg["free"])
                            P.wait("vector", "DV", suff_rdy)
                            dvc = P.op("vector", lambda e, _ci=ci, _l=l, _cg=cg: e.scalar_tensor_tensor(
                                out=_cg["buf"][:], in0=A["pctx"][_ci][:],
                                scalar=A["suff"][:, _l:_l + 1], in1=A["zb"][:],
                                op0=Op.add, op1=Op.mult), inc="DV")
                            pctx_free[ci] = ("DV", dvc)
                            P.wait("gpsimd", "DV", dvc)
                            P.op("gpsimd", lambda e, _l=l, _h=h, _cg=cg: e.dma_start(
                                out=ctx_sw[_l * 128:(_l + 1) * 128, _h * TS:(_h + 1) * TS],
                                in_=_cg["buf"][:]), inc=cg["sem"], n=16)
                            cg["free"] = (cg["sem"], P.cnt[cg["sem"]])
                    kh["free"] = ("PE", P.cnt["PE"])
                    vh["free"] = ("PE", P.cnt["PE"])

            if upto >= 4:
                run_phase(at_alloc, at_emit)

        def wo_alloc(st):
            return {
                "wbuf": sb(st, "wo_wbuf", [128, NCH * D], BF),
                "ctx": [sb(st, f"wo_cx{i}", [128, H * TS], BF) for i in range(2)],
                "xo": [sb(st, f"wo_xo{i}", [128, TS], F32) for i in range(4)],
                "r": sb(st, "wo_r", [128, NCH * TS], F32),
                "rb": [sb(st, f"wo_rb{i}", [128, TS], BF) for i in range(2)],
                "sq": [sb(st, f"wo_sq{i}", [128, TS], BF) for i in range(2)],
                "stat": sb(st, "wo_st", [1, 6 * TS], F32),
                "mb": sb(st, "wo_mb", [128, TS], F32),
                "rstdb": sb(st, "wo_rs", [128, TS], F32),
                "x1f": [sb(st, f"wo_x1f{i}", [128, TS], F32) for i in range(2)],
                "x1b": [sb(st, f"wo_x1b{i}", [128, TS], BF) for i in range(2)],
                "pao": [ps(st, f"wo_pa{i}") for i in range(2)],
                "ps1": ps(st, "wo_ps1"),
                "ps2": ps(st, "wo_ps2"),
                "pb1": ps(st, "wo_pb1"),
                "pb2": ps(st, "wo_pb2"),
            }

        def ln_tail(P, A, eps, state):
            st_ = A["stat"]
            P.op("vector", lambda e: e.tensor_scalar(
                out=st_[:, 0:TS], in0=A["ps1"][0:1, :], scalar1=1.0 / D, scalar2=0.0, op0=Op.mult, op1=Op.add))
            P.op("vector", lambda e: e.tensor_tensor(
                out=st_[:, 2 * TS:3 * TS], in0=st_[:, 0:TS], in1=st_[:, 0:TS],
                op=Op.mult))
            P.op("vector", lambda e: e.scalar_tensor_tensor(
                out=st_[:, 3 * TS:4 * TS], in0=A["ps2"][0:1, :], scalar=1.0 / D,
                in1=st_[:, 2 * TS:3 * TS], op0=Op.mult, op1=Op.subtract))
            dvv = P.op("vector", lambda e, _e=eps: e.tensor_scalar(
                out=st_[:, 4 * TS:5 * TS], in0=st_[:, 3 * TS:4 * TS], scalar1=_e,
                scalar2=0.0, op0=Op.add, op1=Op.add), inc="DV")
            state["ps_free"] = ("DV", dvv)
            P.wait("scalar", "DV", dvv)
            P.op("scalar", lambda e: e.activation(
                out=st_[:, 5 * TS:6 * TS], in_=st_[:, 4 * TS:5 * TS], func=AF.Ln))
            ac = P.op("scalar", lambda e: e.activation(
                out=st_[:, TS:2 * TS], in_=st_[:, 5 * TS:6 * TS], func=AF.Exp,
                scale=-0.5), inc="AC")
            if state.get("pb_free") is not None:
                P.wait("tensor", *state["pb_free"])
            P.wait("tensor", "AC", ac)
            P.op("tensor", lambda e: e.matmul(
                out=A["pb1"][:], lhsT=ones_f[:], rhs=st_[:, 0:TS],
                start=True, stop=True))
            peb = P.op("tensor", lambda e: e.matmul(
                out=A["pb2"][:], lhsT=ones_f[:], rhs=st_[:, TS:2 * TS],
                start=True, stop=True), inc="PE")
            P.wait("vector", "PE", peb)
            P.op("vector", lambda e: e.tensor_copy(out=A["mb"][:], in_=A["pb1"][:]))
            dvb = P.op("vector", lambda e: e.tensor_copy(
                out=A["rstdb"][:], in_=A["pb2"][:]), inc="DV")
            state["pb_free"] = ("DV", dvb)
            return dvb

        def wo_emit(P, A):
            P.wait("sync", "SC", P.cnt["SC"])
            ctx_p = Pool(P, A["ctx"], ["qb0", "qb1"])
            xo_p = Pool(P, A["xo"], ["qc0", "qc1", "qc2", "qc3"])
            x1f_p = Pool(P, A["x1f"], ["w0", "w1"])
            x1b_p = Pool(P, A["x1b"], ["w2", "w3"])
            pao_free = [None, None]
            rb_free = [None, None]
            state = {"ps_free": None, "pb_free": None}
            pend_st = []

            def flush_st(limit):
                while len(pend_st) > limit:
                    acs, ri, first, last = pend_st.pop(0)
                    if first and state["ps_free"] is not None:
                        P.wait("tensor", *state["ps_free"])
                    P.wait("tensor", "AC", acs)
                    P.op("tensor", lambda e, _ri=ri, _f=first, _la=last: e.matmul(
                        out=A["ps1"][0:1, :], lhsT=ones_bf[:, 0:1], rhs=A["rb"][_ri][:],
                        start=_f, stop=_la))
                    pst = P.op("tensor", lambda e, _ri=ri, _f=first, _la=last: e.matmul(
                        out=A["ps2"][0:1, :], lhsT=ones_bf[:, 0:1], rhs=A["sq"][_ri][:],
                        start=_f, stop=_la), inc="PE")
                    rb_free[ri] = ("PE", pst)
            P.sem("qa0")
            P.active_pool_sems.add("qa0")
            P.op("sync", lambda e: e.dma_start(
                out=A["wbuf"][:], in_=wbf_scr[3 * 128:4 * 128, :]), inc="qa0", n=16)
            w_rdy_sem = ("qa0", P.cnt["qa0"])
            for l in range(NL):
                cx = ctx_p.load("sync",
                                lambda _l=l: ctx_sw[_l * 128:(_l + 1) * 128, :])
                for m in range(NCH):
                    pi = m % 2
                    flush_st(1)
                    if pao_free[pi] is not None:
                        P.wait("tensor", *pao_free[pi])
                    P.wait("tensor", cx["sem"], cx["ready"])
                    P.wait("tensor", *w_rdy_sem)
                    for c in range(NCH):
                        P.op("tensor", lambda e, _c=c, _m=m, _pi=pi, _x=cx: e.matmul(
                            out=A["pao"][_pi][:],
                            lhsT=A["wbuf"][:, _c * D + _m * 128:_c * D + _m * 128 + 128],
                            rhs=_x["buf"][:, _c * TS:(_c + 1) * TS],
                            start=(_c == 0), stop=(_c == NCH - 1)),
                            inc="PE" if c == NCH - 1 else None)
                    pev = P.cnt["PE"]
                    xo = xo_p.load("sync",
                                   lambda _l=l, _m=m: xoTf_sw[
                                       (_l * 16 + _m) * 128:(_l * 16 + _m) * 128 + 128, :])
                    P.wait("vector", "PE", pev)
                    P.wait("vector", xo["sem"], xo["ready"])
                    dvr = P.op("vector", lambda e, _pi=pi, _m=m, _xo=xo: e.tensor_tensor(
                        out=A["r"][:, _m * TS:(_m + 1) * TS], in0=A["pao"][_pi][:],
                        in1=_xo["buf"][:], op=Op.add), inc="DV")
                    pao_free[pi] = ("DV", dvr)
                    xo["free"] = ("DV", dvr)
                    ri = m % 2
                    if rb_free[ri] is not None:
                        P.wait("vector", *rb_free[ri])
                    dvrb = P.op("vector", lambda e, _m=m, _ri=ri: e.tensor_copy(
                        out=A["rb"][_ri][:], in_=A["r"][:, _m * TS:(_m + 1) * TS]),
                        inc="DV")
                    P.wait("scalar", "DV", dvrb)
                    acs = P.op("scalar", lambda e, _ri=ri: e.activation(
                        out=A["sq"][_ri][:], in_=A["rb"][_ri][:], func=AF.Square),
                        inc="AC")
                    pend_st.append((acs, ri, m == 0, m == NCH - 1))
                flush_st(0)
                P.wait("vector", "PE", P.cnt["PE"])
                bc_rdy = ln_tail(P, A, EPS_LN, state)
                for m in range(NCH):
                    xf = x1f_p.next()
                    if xf["free"] is not None:
                        P.wait("vector", *xf["free"])
                    xb = x1b_p.next()
                    if xb["free"] is not None:
                        P.wait("vector", *xb["free"])
                    P.wait("vector", "DV", bc_rdy)
                    P.op("vector", lambda e, _m=m, _xf=xf: e.tensor_tensor(
                        out=_xf["buf"][:], in0=A["r"][:, _m * TS:(_m + 1) * TS],
                        in1=A["mb"][:], op=Op.subtract))
                    P.op("vector", lambda e, _xf=xf: e.tensor_tensor(
                        out=_xf["buf"][:], in0=_xf["buf"][:], in1=A["rstdb"][:],
                        op=Op.mult))
                    dvx = P.op("vector", lambda e, _xf=xf, _xb=xb: e.tensor_copy(
                        out=_xb["buf"][:], in_=_xf["buf"][:]), inc="DV")
                    P.wait("gpsimd", "DV", dvx)
                    P.op("gpsimd", lambda e, _l=l, _m=m, _xf=xf: e.dma_start(
                        out=x1f_sw[(_l * 16 + _m) * 128:(_l * 16 + _m) * 128 + 128, :],
                        in_=_xf["buf"][:]), inc=xf["sem"], n=16)
                    xf["free"] = (xf["sem"], P.cnt[xf["sem"]])
                    P.op("gpsimd", lambda e, _l=l, _m=m, _xb=xb: e.dma_start(
                        out=x1b_sw[_l * 128:(_l + 1) * 128, _m * TS:(_m + 1) * TS],
                        in_=_xb["buf"][:]), inc=xb["sem"], n=16)
                    xb["free"] = (xb["sem"], P.cnt[xb["sem"]])
                cx["free"] = ("PE", P.cnt["PE"])

        if upto >= 5:
            run_phase(wo_alloc, wo_emit)

        for l in range(NL):
            if upto < 6:
                break
            with ExitStack() as sl:
                h_strip = sb(sl, f"f_h{l}", [128, 64 * TS], BF)
                invb = sb(sl, f"f_inv{l}", [128, TS], F32)

                def fp_alloc(st, _l=l):
                    return {
                        "x1s": [sb(st, f"f_x1_{_l}", [128, 16 * TS], BF)],
                        "wg": [sb(st, f"f_wg{i}_{_l}", [128, 16 * TS], BF) for i in range(2)],
                        "wu": [sb(st, f"f_wu{i}_{_l}", [128, 16 * TS], BF) for i in range(2)],
                        "sil": [sb(st, f"f_sil{i}_{_l}", [128, TS], F32) for i in range(2)],
                        "sq": [sb(st, f"f_sq{i}_{_l}", [128, TS], BF) for i in range(4)],
                        "s2": sb(st, f"f_s2_{_l}", [1, 3 * TS], F32),
                        "pg": [ps(st, f"f_pg{i}_{_l}") for i in range(2)],
                        "pu": [ps(st, f"f_pu{i}_{_l}") for i in range(2)],
                        "phs": ps(st, f"f_phs_{_l}"),
                        "pib": ps(st, f"f_pib_{_l}"),
                    }

                def fp_emit(P, A, _l=l):
                    P.wait("sync", "SC", P.cnt["SC"])
                    x1_p = Pool(P, A["x1s"], ["qa0"])
                    wg_p = Pool(P, A["wg"], ["qb0", "qb1"])
                    wu_p = Pool(P, A["wu"], ["qc0", "qc1"])
                    pg_free = [None, None]
                    pu_free = [None, None]
                    sil_free = [None, None]
                    sq_free = [None, None, None, None]
                    pend_sq = []

                    def flush_sq(limit):
                        while len(pend_sq) > limit:
                            acq, qi, first, last = pend_sq.pop(0)
                            P.wait("tensor", "AC", acq)
                            pq = P.op("tensor", lambda e, _qi=qi, _f=first, _la=last: e.matmul(
                                out=A["phs"][0:1, :], lhsT=ones_bf[:, 0:1],
                                rhs=A["sq"][_qi][:],
                                start=_f, stop=_la), inc="PE")
                            sq_free[qi] = ("PE", pq)
                    P.wait("sync", "CC", 3)
                    x1 = x1_p.load("sync",
                                   lambda: x1b_sw[_l * 128:(_l + 1) * 128, :])
                    for g in range(16):
                        wg = wg_p.load("sync",
                                       lambda _g=g: gath_w[wgu_row(_g):wgu_row(_g) + 128, :])
                        wu = wu_p.load("sync",
                                       lambda _g=g: gath_w[wgu_row(16 + _g):wgu_row(16 + _g) + 128, :])
                        for k in range(4):
                            pi = k % 2
                            flush_sq(2)
                            if pg_free[pi] is not None:
                                P.wait("tensor", *pg_free[pi])
                            if pu_free[pi] is not None:
                                P.wait("tensor", *pu_free[pi])
                            P.wait("tensor", x1["sem"], x1["ready"])
                            P.wait("tensor", wg["sem"], wg["ready"])
                            P.wait("tensor", wu["sem"], wu["ready"])
                            for dc in range(NCH):
                                P.op("tensor", lambda e, _dc=dc, _k=k, _pi=pi, _w=wg, _x=x1: e.matmul(
                                    out=A["pg"][_pi][:],
                                    lhsT=_w["buf"][:, _dc * TS + _k * 128:_dc * TS + _k * 128 + 128],
                                    rhs=_x["buf"][:, _dc * TS:(_dc + 1) * TS],
                                    start=(_dc == 0), stop=(_dc == NCH - 1)),
                                    inc="PE" if dc == NCH - 1 else None)
                            peg = P.cnt["PE"]
                            for dc in range(NCH):
                                P.op("tensor", lambda e, _dc=dc, _k=k, _pi=pi, _w=wu, _x=x1: e.matmul(
                                    out=A["pu"][_pi][:],
                                    lhsT=_w["buf"][:, _dc * TS + _k * 128:_dc * TS + _k * 128 + 128],
                                    rhs=_x["buf"][:, _dc * TS:(_dc + 1) * TS],
                                    start=(_dc == 0), stop=(_dc == NCH - 1)),
                                    inc="PE" if dc == NCH - 1 else None)
                            peu = P.cnt["PE"]
                            P.wait("scalar", "PE", peg)
                            if sil_free[pi] is not None:
                                P.wait("scalar", *sil_free[pi])
                            acg = P.op("scalar", lambda e, _pi=pi: e.activation(
                                out=A["sil"][_pi][:], in_=A["pg"][_pi][:], func=AF.Silu),
                                inc="AC")
                            pg_free[pi] = ("AC", acg)
                            P.wait("vector", "AC", acg)
                            P.wait("vector", "PE", peu)
                            hcol = (g * 4 + k) * TS
                            dvh = P.op("vector", lambda e, _pi=pi, _hc=hcol: e.tensor_tensor(
                                out=h_strip[:, _hc:_hc + TS], in0=A["sil"][_pi][:],
                                in1=A["pu"][_pi][:], op=Op.mult), inc="DV")
                            pu_free[pi] = ("DV", dvh)
                            sil_free[pi] = ("DV", dvh)
                            P.wait("scalar", "DV", dvh)
                            qi = (g * 4 + k) % 4
                            if sq_free[qi] is not None:
                                P.wait("scalar", *sq_free[qi])
                            acq = P.op("scalar", lambda e, _qi=qi, _hc=hcol: e.activation(
                                out=A["sq"][_qi][:], in_=h_strip[:, _hc:_hc + TS],
                                func=AF.Square), inc="AC")
                            pend_sq.append((acq, qi, g == 0 and k == 0,
                                            g == 15 and k == 3))
                        wg["free"] = ("PE", P.cnt["PE"])
                        wu["free"] = ("PE", P.cnt["PE"])
                    flush_sq(0)
                    x1["free"] = ("PE", P.cnt["PE"])
                    P.wait("vector", "PE", P.cnt["PE"])
                    dvs = P.op("vector", lambda e: e.tensor_scalar(
                        out=A["s2"][:, 0:TS], in0=A["phs"][0:1, :], scalar1=1.0 / HF,
                        scalar2=EPS_RMS, op0=Op.mult, op1=Op.add), inc="DV")
                    P.wait("scalar", "DV", dvs)
                    P.op("scalar", lambda e: e.activation(
                        out=A["s2"][:, TS:2 * TS], in_=A["s2"][:, 0:TS], func=AF.Ln))
                    aci = P.op("scalar", lambda e: e.activation(
                        out=A["s2"][:, 2 * TS:3 * TS], in_=A["s2"][:, TS:2 * TS],
                        func=AF.Exp, scale=-0.5), inc="AC")
                    P.wait("tensor", "AC", aci)
                    pei = P.op("tensor", lambda e: e.matmul(
                        out=A["pib"][:], lhsT=ones_f[:], rhs=A["s2"][:, 2 * TS:3 * TS],
                        start=True, stop=True), inc="PE")
                    P.wait("vector", "PE", pei)
                    P.op("vector", lambda e: e.tensor_copy(out=invb[:], in_=A["pib"][:]),
                         inc="DV")

                run_phase(fp_alloc, fp_emit)
                h_rdy = P.cnt["DV"]

                def pj_alloc(st, _l=l):
                    return {
                        "wpm": [sb(st, f"f_wp{i}_{_l}", [128, 64 * 128], BF) for i in range(2)],
                        "x1f": [sb(st, f"f_x1f{i}_{_l}", [128, TS], F32) for i in range(2)],
                        "r2": sb(st, f"f_r2_{_l}", [128, NCH * TS], F32),
                        "rb": [sb(st, f"f_rb{i}_{_l}", [128, TS], BF) for i in range(2)],
                        "sq": [sb(st, f"f_sq2{i}_{_l}", [128, TS], BF) for i in range(2)],
                        "stat": sb(st, f"f_st_{_l}", [1, 6 * TS], F32),
                        "mb": sb(st, f"f_mb_{_l}", [128, TS], F32),
                        "rstdb": sb(st, f"f_rs_{_l}", [128, TS], F32),
                        "onv": [sb(st, f"f_on{i}_{_l}", [128, TS], F32) for i in range(2)],
                        "ostg": [sb(st, f"f_og{i}_{_l}", [128, D], F32) for i in range(4)],
                        "pp": [ps(st, f"f_pp{i}_{_l}") for i in range(2)],
                        "ps1": ps(st, f"f_ps1_{_l}"),
                        "ps2": ps(st, f"f_ps2_{_l}"),
                        "pb1": ps(st, f"f_pb1_{_l}"),
                        "pb2": ps(st, f"f_pb2_{_l}"),
                        "pot": [ps(st, f"f_pt{i}_{_l}") for i in range(2)],
                    }

                def pj_emit(P, A, _l=l, _h_rdy=h_rdy):
                    P.wait("sync", "SC", P.cnt["SC"])
                    wpm_p = Pool(P, A["wpm"], ["qa0", "qa1"])
                    x1f_p = Pool(P, A["x1f"], ["qb0", "qb1"])
                    ostg = Pool(P, A["ostg"], ["w0", "w1", "w2", "w3"])
                    pp_free = [None, None]
                    rb_free = [None, None]
                    onv_free = [None, None]
                    pot_free = [None, None]
                    state = {"ps_free": None, "pb_free": None}
                    pend_st = []

                    def flush_st(limit):
                        while len(pend_st) > limit:
                            acs, ri, first, last = pend_st.pop(0)
                            if first and state["ps_free"] is not None:
                                P.wait("tensor", *state["ps_free"])
                            P.wait("tensor", "AC", acs)
                            P.op("tensor", lambda e, _ri=ri, _f=first, _la=last: e.matmul(
                                out=A["ps1"][0:1, :], lhsT=ones_bf[:, 0:1],
                                rhs=A["rb"][_ri][:], start=_f, stop=_la))
                            pst = P.op("tensor", lambda e, _ri=ri, _f=first, _la=last: e.matmul(
                                out=A["ps2"][0:1, :], lhsT=ones_bf[:, 0:1],
                                rhs=A["sq"][_ri][:], start=_f, stop=_la), inc="PE")
                            rb_free[ri] = ("PE", pst)
                    for m in range(NCH):
                        P.wait("sync", "CC", 3)
                        wp = wpm_p.load("sync",
                                        lambda _m=m: gath_w[wp_row(_m):wp_row(_m) + 128, :])
                        pi = m % 2
                        flush_st(1)
                        if pp_free[pi] is not None:
                            P.wait("tensor", *pp_free[pi])
                        P.wait("tensor", wp["sem"], wp["ready"])
                        P.wait("tensor", "DV", _h_rdy)
                        for hc in range(64):
                            P.op("tensor", lambda e, _hc=hc, _pi=pi, _w=wp: e.matmul(
                                out=A["pp"][_pi][:],
                                lhsT=_w["buf"][:, _hc * 128:(_hc + 1) * 128],
                                rhs=h_strip[:, _hc * TS:(_hc + 1) * TS],
                                start=(_hc == 0), stop=(_hc == 63)),
                                inc="PE" if hc == 63 else None)
                        pev = P.cnt["PE"]
                        wp["free"] = ("PE", pev)
                        xf = x1f_p.load("sync",
                                        lambda _m=m: x1f_sw[
                                            (_l * 16 + _m) * 128:(_l * 16 + _m) * 128 + 128, :])
                        P.wait("vector", "PE", pev)
                        P.wait("vector", xf["sem"], xf["ready"])
                        P.op("vector", lambda e, _pi=pi, _m=m: e.tensor_tensor(
                            out=A["r2"][:, _m * TS:(_m + 1) * TS], in0=A["pp"][_pi][:],
                            in1=invb[:], op=Op.mult))
                        dvr = P.op("vector", lambda e, _m=m, _xf=xf: e.tensor_tensor(
                            out=A["r2"][:, _m * TS:(_m + 1) * TS],
                            in0=A["r2"][:, _m * TS:(_m + 1) * TS],
                            in1=_xf["buf"][:], op=Op.add), inc="DV")
                        pp_free[pi] = ("DV", dvr)
                        xf["free"] = ("DV", dvr)
                        ri = m % 2
                        if rb_free[ri] is not None:
                            P.wait("vector", *rb_free[ri])
                        dvrb = P.op("vector", lambda e, _m=m, _ri=ri: e.tensor_copy(
                            out=A["rb"][_ri][:], in_=A["r2"][:, _m * TS:(_m + 1) * TS]),
                            inc="DV")
                        P.wait("scalar", "DV", dvrb)
                        acs = P.op("scalar", lambda e, _ri=ri: e.activation(
                            out=A["sq"][_ri][:], in_=A["rb"][_ri][:], func=AF.Square),
                            inc="AC")
                        pend_st.append((acs, ri, m == 0, m == NCH - 1))
                    flush_st(0)
                    P.wait("vector", "PE", P.cnt["PE"])
                    bc_rdy = ln_tail(P, A, EPS_LN, state)
                    ogs = [ostg.next() for _ in range(4)]
                    for og in ogs:
                        if og["free"] is not None:
                            P.wait("vector", *og["free"])
                    for m in range(NCH):
                        oi = m % 2
                        P.wait("vector", "DV", bc_rdy)
                        if onv_free[oi] is not None:
                            P.wait("vector", *onv_free[oi])
                        P.op("vector", lambda e, _m=m, _oi=oi: e.tensor_tensor(
                            out=A["onv"][_oi][:], in0=A["r2"][:, _m * TS:(_m + 1) * TS],
                            in1=A["mb"][:], op=Op.subtract))
                        dvo = P.op("vector", lambda e, _oi=oi: e.tensor_tensor(
                            out=A["onv"][_oi][:], in0=A["onv"][_oi][:], in1=A["rstdb"][:],
                            op=Op.mult), inc="DV")
                        for stt in range(4):
                            ti = (m * 4 + stt) % 2
                            if pot_free[ti] is not None:
                                P.wait("tensor", *pot_free[ti])
                            P.wait("tensor", "DV", dvo)
                            pet = P.op("tensor", lambda e, _oi=oi, _st=stt, _ti=ti: e.transpose(
                                out=A["pot"][_ti][:, 0:128],
                                in_=A["onv"][_oi][:, _st * 128:(_st + 1) * 128],
                                identity=id_f[:]), inc="PE")
                            P.wait("vector", "PE", pet)
                            dve = P.op("vector", lambda e, _ti=ti, _st=stt, _m=m: e.tensor_copy(
                                out=ogs[_st]["buf"][:, _m * 128:(_m + 1) * 128],
                                in_=A["pot"][_ti][:, 0:128]), inc="DV")
                            pot_free[ti] = ("DV", dve)
                        onv_free[oi] = ("PE", P.cnt["PE"])
                    P.wait("gpsimd", "DV")
                    for stt in range(4):
                        P.op("gpsimd", lambda e, _st=stt: e.dma_start(
                            out=out[_l * TS + _st * 128:_l * TS + (_st + 1) * 128, :],
                            in_=ogs[_st]["buf"][:]), inc=ogs[stt]["sem"], n=16)
                        ogs[stt]["free"] = (ogs[stt]["sem"], P.cnt[ogs[stt]["sem"]])

                run_phase(pj_alloc, pj_emit)

        def fin_emit(P, A):
            P.wait("gpsimd", "SC", P.cnt["SC"])
            P.wait("gpsimd", "IN", P.cnt["IN"])

        run_phase(lambda st: {}, fin_emit)

    return nc


def make_in_maps(x, Wq, Wk, Wv, Wo, wgu, wp):
    wgu_shards = [np.ascontiguousarray(np.concatenate(
        [wgu[(c + 8 * k) * 512:(c + 8 * k + 1) * 512] for k in range(4)], axis=0))
        for c in range(8)]
    wp_shards = [np.ascontiguousarray(np.concatenate(
        [wp[(c + 8 * k) * 128:(c + 8 * k + 1) * 128] for k in range(2)], axis=0))
        for c in range(8)]
    in_maps = []
    ones_bf = np.ones((128, 128), dtype=NPBF)
    ones_f = np.ones((1, 128), dtype=np.float32)
    id_bf = np.eye(128, dtype=np.float32).astype(NPBF)
    id_f = np.eye(128, dtype=np.float32)
    pp = np.arange(128)[:, None]
    tt = np.arange(TS)[None, :]
    for c in range(8):
        b, half = c // 2, c % 2
        xb = np.ascontiguousarray(x[b])
        x_own = np.ascontiguousarray(
            np.concatenate([xb[(2 * l + half) * TS:(2 * l + half + 1) * TS]
                            for l in range(NL)], axis=0))
        mb = np.zeros((128, NL * 8 * TS), dtype=NPBF)
        for l in range(NL):
            E = 2 * l + half
            for rel in range(8):
                j = 8 * l + rel
                keep = (j * 128 + pp) <= (E * TS + tt)
                mb[:, (l * 8 + rel) * TS:(l * 8 + rel + 1) * TS] = keep.astype(NPBF)
        in_maps.append({
            "x_own": x_own,
            "wq": Wq, "wk": Wk, "wv": Wv, "wo": Wo,
            "wgu_my": wgu_shards[c], "wp_my": wp_shards[c],
            "c_mask": mb,
            "c_ones_bf": ones_bf,
            "c_ones_f": ones_f,
            "c_id_bf": id_bf,
            "c_id_f": id_f,
        })
    return in_maps


def kernel(L0_attn_x, L0_attn_Wq, L0_attn_Wk, L0_attn_Wv, L0_attn_Wo,
           L0_ffn_w_gate_up, L0_ffn_w_proj):
    x = np.ascontiguousarray(L0_attn_x, dtype=np.float32)
    nc = build()
    in_maps = make_in_maps(
        x,
        np.ascontiguousarray(L0_attn_Wq, dtype=np.float32),
        np.ascontiguousarray(L0_attn_Wk, dtype=np.float32),
        np.ascontiguousarray(L0_attn_Wv, dtype=np.float32),
        np.ascontiguousarray(L0_attn_Wo, dtype=np.float32),
        np.ascontiguousarray(L0_ffn_w_gate_up, dtype=np.float32),
        np.ascontiguousarray(L0_ffn_w_proj, dtype=np.float32))
    res = run_bass_kernel_spmd(nc, in_maps, core_ids=list(range(8)))
    outf = np.empty((B, T, D), dtype=np.float32)
    for c in range(8):
        b, half = c // 2, c % 2
        oc = res.results[c]["out"]
        for l in range(NL):
            E = 2 * l + half
            outf[b, E * TS:(E + 1) * TS] = oc[l * TS:(l + 1) * TS]
    return outf


# version 17
# speedup vs baseline: 1.0244x; 1.0244x over previous
import numpy as np
import ml_dtypes
from contextlib import ExitStack

import concourse.bass as bass
import concourse.mybir as mybir
from concourse.bass_utils import run_bass_kernel_spmd
from concourse.alu_op_type import AluOpType as Op

BF = mybir.dt.bfloat16
F32 = mybir.dt.float32
AF = mybir.ActivationFunctionType
AX = mybir.AxisListType

C_THR = 0.5009765625
B, T, D, H, DH = 4, 4096, 2048, 16, 128
TS = 512
NL = 4
TOK = NL * TS
NCH = D // 128
HF = 8192
EPS_LN = 1e-5
EPS_RMS = 1.1920929e-7
INV64 = 1.0 / 64.0
ENGINES = ("sync", "vector", "scalar", "tensor", "gpsimd")

NPBF = ml_dtypes.bfloat16


class Prog:

    def __init__(self, nc, stack):
        self.nc = nc
        self.stack = stack
        self.cur = None
        self.cnt = {}
        self.sems = {}
        self.active_pool_sems = set()

    def sem(self, name):
        if name not in self.sems:
            self.sems[name] = self.stack.enter_context(self.nc.semaphore(name))
        if name not in self.cnt:
            self.cnt[name] = 0
        return self.sems[name]

    AUTOFENCE = {"vector": "DV", "scalar": "AC"}

    def op(self, engname, fn, inc=None, n=1, nofence=False):
        auto = self.AUTOFENCE.get(engname)
        if auto is not None and not nofence:
            self.wait(engname, auto)
            if inc is None:
                inc = auto
        if inc is not None:
            self.sem(inc)
        if self.cur == engname:
            instr = fn(getattr(self.nc, engname))
            if inc is not None:
                instr.then_inc(self.sems[inc], n)
        if inc is not None:
            self.cnt[inc] += n
            return self.cnt[inc]
        return None

    def wait(self, engname, semname, val=None):
        self.sem(semname)
        v = self.cnt[semname] if val is None else val
        if v <= 0:
            return
        if self.cur == engname:
            getattr(self.nc, engname).wait_ge(self.sems[semname], v)

    def waitl(self, engname, items):
        for it in items:
            if it is not None:
                self.wait(engname, it[0], it[1])

    def drain_pools(self, engname):
        for sem in sorted(self.active_pool_sems):
            self.wait(engname, sem, self.cnt[sem])
        self.active_pool_sems = set()


class Pool:

    def __init__(self, P, bufs, semnames):
        self.P = P
        self.slots = [{"buf": b, "sem": s, "ready": 0, "free": None}
                      for b, s in zip(bufs, semnames)]
        for s in self.slots:
            if s["sem"] is not None:
                P.sem(s["sem"])
                P.active_pool_sems.add(s["sem"])
        self.i = 0

    def next(self):
        s = self.slots[self.i % len(self.slots)]
        self.i += 1
        return s

    def load(self, q, src_fn, waitl=()):
        s = self.next()
        if s["free"] is not None:
            self.P.waitl(q, s["free"] if isinstance(s["free"], list) else [s["free"]])
        for w in waitl:
            self.P.wait(q, *w)
        self.P.op(q, lambda e: e.dma_start(out=s["buf"][:], in_=src_fn()),
                  inc=s["sem"], n=16)
        s["ready"] = self.P.cnt[s["sem"]]
        return s


def mid3(ap2, mid_step, mid_count):
    return bass.AP(ap2.tensor, ap2.offset, [ap2.ap[0], [mid_step, mid_count], ap2.ap[1]])


def bview(ap2, bcast):
    return bass.AP(ap2.tensor, ap2.offset, list(ap2.ap) + [[0, bcast]])


def build(taps=(), upto=99):
    nc = bass.Bass("TRN2", target_bir_lowering=False)

    def dram(name, shape, dt, kind=None):
        if name in taps:
            kind = "ExternalOutput"
        if kind is None:
            return nc.dram_tensor(name, shape, dt).ap()
        return nc.dram_tensor(name, shape, dt, kind=kind).ap()

    x_own = dram("x_own", [TOK, D], F32, "ExternalInput")
    wq_in = dram("wq", [D, D], F32, "ExternalInput")
    wk_in = dram("wk", [D, D], F32, "ExternalInput")
    wv_in = dram("wv", [D, D], F32, "ExternalInput")
    wo_in = dram("wo", [D, D], F32, "ExternalInput")
    wgu_my = dram("wgu_my", [4 * 512, D], F32, "ExternalInput")
    wp_my = dram("wp_my", [2 * 128, HF], F32, "ExternalInput")
    c_mask = dram("c_mask", [128, NL * 8 * TS], BF, "ExternalInput")
    c_ones_bf = dram("c_ones_bf", [128, 128], BF, "ExternalInput")
    c_ones_f = dram("c_ones_f", [1, 128], F32, "ExternalInput")
    c_id_bf = dram("c_id_bf", [128, 128], BF, "ExternalInput")
    c_id_f = dram("c_id_f", [128, 128], F32, "ExternalInput")
    out = dram("out", [TOK, D], F32, "ExternalOutput")

    wbf_scr = dram("wbf_scr", [4 * 128, NCH * D], BF)
    contrib_w = dram("contrib_w", [6 * 128, 16 * TS], BF)
    gath_w = nc.dram_tensor("gath_w", [8 * 6 * 128, 16 * TS], BF,
                            addr_space="Shared").ap()

    def wgu_row(g):
        return ((g % 8) * 6 + g // 8) * 128

    def wp_row(m):
        return ((m % 8) * 6 + 4 + m // 8) * 128
    xoT_sw = dram("xoT_sw", [NL * 128, 16 * TS], BF)
    xoTf_sw = dram("xoTf_sw", [NL * 16 * 128, TS], F32)
    kc_scr = dram("kc_scr", [D, TOK], BF)
    vc_scr = dram("vc_scr", [H * 128, TOK], BF)
    kg_scr = dram("kg_scr", [2 * D, TOK], BF)
    vg_scr = dram("vg_scr", [2 * H * 128, TOK], BF)
    ctx_sw = dram("ctx_sw", [NL * 128, H * TS], BF)
    x1f_sw = dram("x1f_sw", [NL * 16 * 128, TS], F32)
    x1b_sw = dram("x1b_sw", [NL * 128, 16 * TS], BF)

    gstack = ExitStack()
    with gstack:
        P = Prog(nc, gstack)
        for s in ("IN", "SC", "PE", "DV", "AC"):
            P.sem(s)

        def sb(st, name, shape, dt, side="left"):
            return st.enter_context(nc.sbuf_tensor(name, shape, dt, side=side))

        def ps(st, name, dt=F32):
            cols = TS if dt == F32 else 1024
            return st.enter_context(nc.psum_tensor(name, [128, cols], dt))

        def run_phase(allocf, emitf):
            with ExitStack() as st:
                bufs = allocf(st)
                with nc.Block() as blk:
                    snap = dict(P.cnt)
                    for en in ENGINES:
                        P.cnt = dict(snap)
                        P.cur = en
                        P.active_pool_sems = set()

                        def body(eng, _f=emitf, _b=bufs):
                            _f(P, _b)
                            P.drain_pools("gpsimd")
                        getattr(blk, en)(body)
                    P.cur = None

        ones_bf = sb(gstack, "ones_bf", [128, 128], BF, side="right")
        ones_f = sb(gstack, "ones_f", [1, 128], F32, side="right")
        id_bf = sb(gstack, "id_bf", [128, 128], BF, side="right")
        id_f = sb(gstack, "id_f", [128, 128], F32, side="right")

        def ph_consts(P, A):
            for dst, src in ((ones_bf, c_ones_bf), (ones_f, c_ones_f),
                             (id_bf, c_id_bf), (id_f, c_id_f)):
                P.op("sync", lambda e, d=dst, s=src: e.dma_start(out=d[:], in_=s),
                     inc="IN", n=16)
            P.wait("sync", "IN")

        run_phase(lambda st: {}, ph_consts)

        def q_alloc(st):
            return {
                "wfin": [sb(st, f"q_wf{i}", [128, D], F32) for i in range(2)],
                "wcst": [sb(st, f"q_wc{i}", [128, D], BF) for i in range(2)],
                "win": [sb(st, f"q_win{i}", [128, 4 * D], F32) for i in range(2)],
                "wb": sb(st, "q_wb", [128, D], BF),
                "t1": sb(st, "q_t1", [128, D], BF),
                "t2": sb(st, "q_t2", [128, D], BF),
                "sums": sb(st, "q_sums", [128, D // 64], F32),
                "sbf": sb(st, "q_sbf", [128, D // 64], BF),
                "thr": sb(st, "q_thr", [128, D // 64], F32),
                "wqt": [sb(st, f"q_wqt{i}", [128, D], BF) for i in range(2)],
                "stage": [sb(st, f"q_stg{i}", [128, 16 * TS], BF) for i in range(2)],
                "psq": [ps(st, f"q_ps{i}", BF) for i in range(2)],
            }

        def q_emit(P, A):
            win = Pool(P, A["win"], ["qa0", "qa1"])
            stage = Pool(P, A["stage"], ["w0", "w1"])
            wfin = Pool(P, A["wfin"], ["qd0", "qd1"])
            wcst = Pool(P, A["wcst"], ["w6", "w7"])
            wjobs = [(wi, c) for wi in range(4) for c in range(NCH)]
            wstate = [0]

            def pump_w(n):
                for _ in range(n):
                    if wstate[0] >= len(wjobs):
                        return
                    wi, c = wjobs[wstate[0]]
                    wstate[0] += 1
                    wt = (wk_in, wv_in, wq_in, wo_in)[wi]
                    s = wfin.load("sync", lambda _c=c, _w=wt: _w[_c * 128:(_c + 1) * 128, :])
                    dstc = wcst.next()
                    if dstc["free"] is not None:
                        P.wait("scalar", *dstc["free"])
                    P.wait("scalar", s["sem"], s["ready"])
                    acw = P.op("scalar", lambda e, _s=s, _d=dstc: e.copy(
                        out=_d["buf"][:], in_=_s["buf"][:]), inc="AC", nofence=True)
                    s["free"] = ("AC", acw)
                    P.wait("gpsimd", "AC", acw)
                    P.op("gpsimd", lambda e, _wi=wi, _c=c, _d=dstc: e.dma_start(
                        out=wbf_scr[_wi * 128:(_wi + 1) * 128, _c * D:(_c + 1) * D],
                        in_=_d["buf"][:]), inc=dstc["sem"], n=16)
                    dstc["free"] = (dstc["sem"], P.cnt[dstc["sem"]])
            psq_free = [None, None]
            wqt_free = [None, None]
            groups = [("wgu", g) for g in range(4)] + [("wp", m) for m in range(2)]
            for gi, (kind, g) in enumerate(groups):
                if kind == "wgu":
                    src = lambda _k=g: bass.AP(wgu_my.tensor, _k * 4 * 128 * D,
                                               [[D, 128], [128 * D, 4], [1, D]])
                else:
                    src = lambda _k=g: bass.AP(wp_my.tensor, _k * 128 * HF,
                                               [[HF, 128], [D, 4], [1, D]])
                ws = win.load("sync", src)
                pump_w(11)
                stg = stage.next()
                if stg["free"] is not None:
                    P.wait("vector", *stg["free"])
                for sub in range(4):
                    P.wait("vector", ws["sem"], ws["ready"])
                    wsub = lambda _s=sub, _b=ws["buf"]: _b[:, _s * D:(_s + 1) * D]
                    P.op("vector", lambda e: e.tensor_copy(out=A["wb"][:], in_=wsub()))
                    P.op("vector", lambda e: e.tensor_reduce(
                        out=A["sums"][:],
                        in_=A["wb"][:].rearrange("p (g k) -> p g k", k=64),
                        axis=AX.X, op=Op.add, apply_absolute_value=True))
                    P.op("vector", lambda e: e.tensor_scalar(
                        out=A["sbf"][:], in0=A["sums"][:], scalar1=INV64,
                        scalar2=1e-8, op0=Op.mult, op1=Op.max))
                    P.op("vector", lambda e: e.tensor_scalar(
                        out=A["thr"][:], in0=A["sbf"][:], scalar1=C_THR, scalar2=0.0, op0=Op.mult, op1=Op.add))
                    thr_b = bview(A["thr"][:], 64)
                    sbf_b = bview(A["sbf"][:], 64)
                    w3 = A["wb"][:].rearrange("p (g k) -> p g k", k=64)
                    t13 = A["t1"][:].rearrange("p (g k) -> p g k", k=64)
                    t23 = A["t2"][:].rearrange("p (g k) -> p g k", k=64)
                    wi = (gi * 4 + sub) % 2
                    wqt = A["wqt"][wi]
                    if wqt_free[wi] is not None:
                        P.wait("vector", *wqt_free[wi])
                    P.op("vector", lambda e: e.tensor_tensor(
                        out=t13, in0=w3, in1=thr_b, op=Op.is_gt))
                    P.op("vector", lambda e: e.scalar_tensor_tensor(
                        out=t23, in0=w3, scalar=-1.0, in1=thr_b,
                        op0=Op.mult, op1=Op.is_gt))
                    P.op("vector", lambda e: e.tensor_tensor(
                        out=A["t1"][:], in0=A["t1"][:], in1=A["t2"][:], op=Op.subtract))
                    P.op("vector", lambda e: e.tensor_tensor(
                        out=t13, in0=t13, in1=sbf_b, op=Op.mult))
                    P.op("vector", lambda e: e.tensor_tensor(
                        out=A["t1"][:], in0=A["t1"][:], in1=A["wb"][:], op=Op.subtract))
                    dvv = P.op("vector", lambda e, _w=wqt: e.tensor_tensor(
                        out=_w[:], in0=A["wb"][:], in1=A["t1"][:], op=Op.add), inc="DV")
                    for k4 in range(4):
                        pi = k4 % 2
                        if psq_free[pi] is not None:
                            P.wait("tensor", *psq_free[pi])
                        P.wait("tensor", "DV", dvv)
                        for kk in range(4):
                            k = k4 * 4 + kk
                            pe = P.op("tensor", lambda e, _k=k, _pi=pi, _kk=kk, _w=wqt: e.transpose(
                                out=A["psq"][_pi][:, _kk * 128:(_kk + 1) * 128],
                                in_=_w[:, _k * 128:(_k + 1) * 128],
                                identity=id_bf[:]), inc="PE")
                        P.wait("vector", "PE", pe)
                        if kind == "wgu":
                            dst_ap = bass.AP(stg["buf"][:].tensor,
                                             stg["buf"][:].offset + (4 * k4 * TS + sub * 128),
                                             [stg["buf"][:].ap[0], [TS, 4], [1, 128]])
                            src_ap = A["psq"][pi][:, 0:TS].rearrange(
                                "p (a b) -> p a b", b=128)
                        else:
                            dst_ap = stg["buf"][:, sub * 2048 + k4 * TS:
                                                sub * 2048 + (k4 + 1) * TS]
                            src_ap = A["psq"][pi][:, 0:TS]
                        dve = P.op("vector", lambda e, _d=dst_ap, _s=src_ap: e.tensor_copy(
                            out=_d, in_=_s), inc="DV")
                        psq_free[pi] = ("DV", dve)
                ws["free"] = ("DV", P.cnt["DV"])
                wqt_free[wi] = ("PE", P.cnt["PE"])
                P.wait("gpsimd", "DV")
                dst = lambda _gi=gi: contrib_w[_gi * 128:(_gi + 1) * 128, :]
                P.op("gpsimd", lambda e, _sg=stg: e.dma_start(out=dst(), in_=_sg["buf"][:]),
                     inc=stg["sem"], n=16)
                stg["free"] = (stg["sem"], P.cnt[stg["sem"]])
            pump_w(64)

        if upto >= 1:
            run_phase(q_alloc, q_emit)

        def xt_alloc(st):
            return {
                "xin": [sb(st, f"x_in{i}", [128, 4 * D], F32) for i in range(2)],
                "xb": sb(st, "x_xb", [128, 4 * D], BF),
                "stg_b": [sb(st, f"x_sb{i}", [128, 16 * TS], BF) for i in range(2)],
                "stg_f": [sb(st, "x_sf0", [128, 16 * TS], F32)],
                "psb": [ps(st, f"x_pb{i}", BF) for i in range(2)],
                "psf": [ps(st, f"x_pf{i}", F32) for i in range(2)],
            }

        def xt_emit(P, A):
            P.wait("sync", "SC", P.cnt["SC"])
            xin = Pool(P, A["xin"], ["qa0", "qa1"])
            stgb_p = Pool(P, A["stg_b"], ["w0", "w1"])
            stgf_p = Pool(P, A["stg_f"], ["w2"])
            psb_free = [None, None]
            psf_free = [None, None]
            jobs = [("own", s) for s in range(NL)]
            for kind, sec in jobs:
                src_t = x_own.tensor
                xs = xin.load("sync",
                              lambda _s=sec, _t=src_t: bass.AP(
                                  _t, _s * TS * D, [[D, 128], [128 * D, 4], [1, D]]))
                P.wait("vector", xs["sem"], xs["ready"])
                dvc = P.op("vector", lambda e, _x=xs: e.tensor_copy(
                    out=A["xb"][:], in_=_x["buf"][:]), inc="DV")
                stgb = stgb_p.next()
                if stgb["free"] is not None:
                    P.wait("vector", *stgb["free"])
                stgf = None
                if kind == "own":
                    stgf = stgf_p.next()
                    if stgf["free"] is not None:
                        P.wait("vector", *stgf["free"])
                for dc in range(NCH):
                    pi = dc % 2
                    if psb_free[pi] is not None:
                        P.wait("tensor", *psb_free[pi])
                    P.wait("tensor", "DV", dvc)
                    for tile in range(4):
                        pe = P.op("tensor", lambda e, _c=tile * D + dc * 128, _p=pi,
                                  _t=tile: e.transpose(
                            out=A["psb"][_p][:, _t * 128:(_t + 1) * 128],
                            in_=A["xb"][:, _c:_c + 128],
                            identity=id_bf[:]), inc="PE")
                    P.wait("vector", "PE", pe)
                    dve = P.op("vector", lambda e, _p=pi, _d=dc * TS, _sg=stgb: e.tensor_copy(
                        out=_sg["buf"][:, _d:_d + TS], in_=A["psb"][_p][:, 0:TS]),
                        inc="DV")
                    psb_free[pi] = ("DV", dve)
                    if kind == "own":
                        if psf_free[pi] is not None:
                            P.wait("tensor", *psf_free[pi])
                        P.wait("tensor", xs["sem"], xs["ready"])
                        for tile in range(4):
                            pe2 = P.op("tensor", lambda e, _c=tile * D + dc * 128, _p=pi,
                                       _t=tile, _x=xs: e.transpose(
                                out=A["psf"][_p][:, _t * 128:(_t + 1) * 128],
                                in_=_x["buf"][:, _c:_c + 128],
                                identity=id_f[:]), inc="PE")
                        P.wait("vector", "PE", pe2)
                        dve2 = P.op("vector", lambda e, _p=pi, _d=dc * TS, _sg=stgf: e.tensor_copy(
                            out=_sg["buf"][:, _d:_d + TS], in_=A["psf"][_p][:, 0:TS]),
                            inc="DV")
                        psf_free[pi] = ("DV", dve2)
                xs["free"] = [("DV", P.cnt["DV"]), ("PE", P.cnt["PE"])]
                P.wait("gpsimd", "DV")
                dstp = lambda _s=sec: xoT_sw[_s * 128:(_s + 1) * 128, :]
                P.op("gpsimd", lambda e, _sg=stgb: e.dma_start(out=dstp(), in_=_sg["buf"][:]),
                     inc=stgb["sem"], n=16)
                stgb["free"] = (stgb["sem"], P.cnt[stgb["sem"]])
                if kind == "own":
                    for dc in range(NCH):
                        row0 = (sec * 16 + dc) * 128
                        P.op("gpsimd", lambda e, _r=row0, _d=dc, _sg=stgf: e.dma_start(
                            out=xoTf_sw[_r:_r + 128, :],
                            in_=_sg["buf"][:, _d * TS:(_d + 1) * TS]),
                            inc=stgf["sem"], n=16)
                    stgf["free"] = (stgf["sem"], P.cnt[stgf["sem"]])

        if upto >= 2:
            run_phase(xt_alloc, xt_emit)

        with ExitStack() as midst:
            qT = sb(midst, "qT", [128, NCH * TOK], BF, side="right")

            def kvq_alloc(st):
                return {
                    "wbuf": sb(st, "kv_wbuf", [128, NCH * D], BF),
                    "xsec": [sb(st, f"kv_xs{i}", [128, 16 * TS], BF) for i in range(2)],
                    "ostg": [sb(st, f"kv_os{i}", [128, TS], BF) for i in range(4)],
                    "vstg": [sb(st, "kv_vs0", [128, 4 * D], BF)],
                    "pk": [ps(st, f"kv_pk{i}") for i in range(2)],
                }

            def kvq_emit(P, A):
                P.wait("sync", "SC", P.cnt["SC"])
                xsec = Pool(P, A["xsec"], ["qb0", "qb1"])
                ostg = Pool(P, A["ostg"], ["w2", "w3", "w4", "w5"])
                vstg = Pool(P, A["vstg"], ["w6"])
                pk_free = [None, None]
                wsem = ["qa0", "qa1", "qd0"]
                wseq = [0]

                def load_w(wi):
                    sem = wsem[wseq[0] % 3]
                    wseq[0] += 1
                    P.sem(sem)
                    P.active_pool_sems.add(sem)
                    P.wait("sync", "PE", P.cnt["PE"])
                    P.op("sync", lambda e, _wi=wi: e.dma_start(
                        out=A["wbuf"][:], in_=wbf_scr[_wi * 128:(_wi + 1) * 128, :]),
                        inc=sem, n=16)
                    P.wait("tensor", sem, P.cnt[sem])
                    return 0

                def proj_pass(src_plane_fn, nsec, evac_fn):
                    for sec in range(nsec):
                        xs = xsec.load("sync", lambda _s=sec: src_plane_fn(_s))
                        for c in range(NCH):
                            pi = c % 2
                            if pk_free[pi] is not None:
                                P.wait("tensor", *pk_free[pi])
                            P.wait("tensor", xs["sem"], xs["ready"])
                            for dc in range(NCH):
                                P.op("tensor", lambda e, _dc=dc, _c=c, _pi=pi, _x=xs: e.matmul(
                                    out=A["pk"][_pi][:],
                                    lhsT=A["wbuf"][:, _dc * D + _c * 128:_dc * D + _c * 128 + 128],
                                    rhs=_x["buf"][:, _dc * TS:(_dc + 1) * TS],
                                    start=(_dc == 0), stop=(_dc == NCH - 1)),
                                    inc="PE" if dc == NCH - 1 else None)
                            pk_free[pi] = evac_fn(sec, c, pi)
                        xs["free"] = ("PE", P.cnt["PE"])

                load_w(1)
                w_rdy = P.cnt["DV"]
                for sec in range(NL):
                    xs = xsec.load("sync",
                                   lambda _s=sec: xoT_sw[_s * 128:(_s + 1) * 128, :])
                    vg = vstg.next()
                    if vg["free"] is not None:
                        P.wait("vector", *vg["free"])
                    for stp in range(4):
                        for cs in range(4):
                            pi = (stp * 4 + cs) % 2
                            if pk_free[pi] is not None:
                                P.wait("tensor", *pk_free[pi])
                            P.wait("tensor", xs["sem"], xs["ready"])
                            for dc in range(NCH):
                                P.op("tensor", lambda e, _dc=dc, _st=stp, _cs=cs, _pi=pi, _x=xs: e.matmul(
                                    out=A["pk"][_pi][:],
                                    lhsT=_x["buf"][:, _dc * TS + _st * 128:_dc * TS + _st * 128 + 128],
                                    rhs=A["wbuf"][:, _dc * D + _cs * TS:_dc * D + (_cs + 1) * TS],
                                    start=(_dc == 0), stop=(_dc == NCH - 1)),
                                    inc="PE" if dc == NCH - 1 else None)
                            pev = P.cnt["PE"]
                            P.wait("vector", "PE", pev)
                            dve = P.op("vector", lambda e, _pi=pi, _st=stp, _cs=cs, _v=vg: e.tensor_copy(
                                out=_v["buf"][:, _st * D + _cs * TS:_st * D + (_cs + 1) * TS],
                                in_=A["pk"][_pi][:]), inc="DV")
                            pk_free[pi] = ("DV", dve)
                    P.wait("gpsimd", "DV")
                    for h in range(H):
                        base = vg["buf"][:, h * 128:h * 128 + 128]
                        P.op("gpsimd", lambda e, _h=h, _s=sec, _b=base: e.dma_start(
                            out=vc_scr[_h * 128:(_h + 1) * 128, _s * TS:(_s + 1) * TS],
                            in_=mid3(_b, D, 4)),
                            inc=vg["sem"], n=16)
                    vg["free"] = (vg["sem"], P.cnt[vg["sem"]])
                    xs["free"] = ("PE", P.cnt["PE"])
                P.wait("gpsimd", "w6", P.cnt.get("w6", 0))
                P.op("gpsimd", lambda e: e.collective_compute(
                    "AllGather", mybir.AluOpType.bypass,
                    replica_groups=[[0, 1], [2, 3], [4, 5], [6, 7]],
                    ins=[vc_scr], outs=[vg_scr]), inc="CC", n=1)

                load_w(0)

                def k_evac(sec, c, pi):
                    pev = P.cnt["PE"]
                    og = ostg.next()
                    if og["free"] is not None:
                        P.wait("vector", *og["free"])
                    P.wait("vector", "PE", pev)
                    dve = P.op("vector", lambda e, _pi=pi, _o=og: e.tensor_copy(
                        out=_o["buf"][:], in_=A["pk"][_pi][:]), inc="DV")
                    P.wait("gpsimd", "DV", dve)
                    P.op("gpsimd", lambda e, _c=c, _s=sec, _o=og: e.dma_start(
                        out=kc_scr[_c * 128:(_c + 1) * 128, _s * TS:(_s + 1) * TS],
                        in_=_o["buf"][:]), inc=og["sem"], n=16)
                    og["free"] = (og["sem"], P.cnt[og["sem"]])
                    return ("DV", dve)

                proj_pass(lambda s: xoT_sw[s * 128:(s + 1) * 128, :], NL, k_evac)
                for _s in ("w2", "w3", "w4", "w5"):
                    P.wait("gpsimd", _s, P.cnt.get(_s, 0))
                P.op("gpsimd", lambda e: e.collective_compute(
                    "AllGather", mybir.AluOpType.bypass,
                    replica_groups=[[0, 1], [2, 3], [4, 5], [6, 7]],
                    ins=[kc_scr], outs=[kg_scr]), inc="CC", n=1)

                load_w(2)

                def q_evac(sec, c, pi):
                    pev = P.cnt["PE"]
                    P.wait("vector", "PE", pev)
                    dve = P.op("vector", lambda e, _pi=pi, _c=c, _s=sec: e.tensor_copy(
                        out=qT[:, _c * TOK + _s * TS:_c * TOK + (_s + 1) * TS],
                        in_=A["pk"][_pi][:]), inc="DV")
                    return ("DV", dve)

                proj_pass(lambda s: xoT_sw[s * 128:(s + 1) * 128, :], NL, q_evac)

            if upto >= 3:
                run_phase(kvq_alloc, kvq_emit)

            def at_alloc(st):
                return {
                    "kh": [sb(st, f"at_k{i}", [128, T], BF) for i in range(2)],
                    "vh": [sb(st, f"at_v{i}", [128, T], BF) for i in range(2)],
                    "att": [sb(st, f"at_a{i}", [128, TS], BF) for i in range(4)],
                    "mtmp": [sb(st, f"at_m{i}", [128, TS], F32) for i in range(2)],
                    "cs": sb(st, "at_cs", [128, 32], F32),
                    "suff": sb(st, "at_sf", [128, NL], F32),
                    "zf": sb(st, "at_zf", [1, TS], F32),
                    "zr": sb(st, "at_zr", [1, TS], F32),
                    "zb": sb(st, "at_zb", [128, TS], F32),
                    "cstg": [sb(st, f"at_cg{i}", [128, TS], BF) for i in range(8)],
                    "psc": [ps(st, f"at_sc{i}") for i in range(4)],
                    "pz": [ps(st, "at_z0")],
                    "pctx": [ps(st, f"at_cx{i}") for i in range(2)],
                    "pcs": ps(st, "at_pcs"),
                    "mask": sb(st, "mask_sb", [128, NL * 8 * TS], BF),
                }

            def at_emit(P, A):
                P.wait("sync", "SC", P.cnt["SC"])
                kh_p = Pool(P, A["kh"], ["qa0", "qa1"])
                vh_p = Pool(P, A["vh"], ["qb0", "qb1"])
                cstg = Pool(P, A["cstg"],
                            ["w0", "w1", "w2", "w3", "w4", "w5", "w6", "w7"])
                for _s in ("w0", "w1"):
                    P.wait("gpsimd", _s, P.cnt.get(_s, 0))
                P.op("gpsimd", lambda e: e.collective_compute(
                    "AllGather", mybir.AluOpType.bypass,
                    replica_groups=[list(range(8))],
                    ins=[contrib_w], outs=[gath_w]), inc="CC", n=1)
                psc_free = [None, None, None, None]
                pz_free = [None]
                pctx_free = [None, None]
                pcs_free = [None]
                mtmp_free = [None, None]
                mask_sb = A["mask"]
                P.op("sync", lambda e: e.dma_start(out=mask_sb[:], in_=c_mask),
                     inc="IN", n=16)
                mask_rdy = P.cnt["IN"]
                P.wait("sync", "IN", mask_rdy)
                for h in range(H):
                    kh = kh_p.next()
                    if kh["free"] is not None:
                        P.waitl("sync", [kh["free"]])
                    P.wait("sync", "CC", 2)
                    for E in range(8):
                        P.op("sync", lambda e, _h=h, _E=E, _k=kh: e.dma_start(
                            out=_k["buf"][:, _E * TS:(_E + 1) * TS],
                            in_=kg_scr[(_E % 2) * D + _h * 128:(_E % 2) * D + _h * 128 + 128,
                                       (_E // 2) * TS:(_E // 2 + 1) * TS]),
                            inc=kh["sem"], n=16)
                    kh["ready"] = P.cnt[kh["sem"]]
                    vh = vh_p.next()
                    if vh["free"] is not None:
                        P.waitl("sync", [vh["free"]])
                    P.wait("sync", "CC", 1)
                    for E in range(8):
                        P.op("sync", lambda e, _h=h, _E=E, _v=vh: e.dma_start(
                            out=_v["buf"][:, _E * 4 * 128:(_E + 1) * 4 * 128],
                            in_=vg_scr[(_E % 2) * D + _h * 128:(_E % 2) * D + _h * 128 + 128,
                                       (_E // 2) * TS:(_E // 2 + 1) * TS]),
                            inc=vh["sem"], n=16)
                    vh["ready"] = P.cnt[vh["sem"]]
                    if pcs_free[0] is not None:
                        P.wait("tensor", *pcs_free[0])
                    P.wait("tensor", vh["sem"], vh["ready"])
                    for j in range(32):
                        P.op("tensor", lambda e, _j=j, _v=vh: e.matmul(
                            out=A["pcs"][:, _j:_j + 1],
                            lhsT=_v["buf"][:, _j * 128:(_j + 1) * 128],
                            rhs=ones_bf[:, 0:1], start=True, stop=True),
                            inc="PE" if j == 31 else None)
                    pev = P.cnt["PE"]
                    P.wait("vector", "PE", pev)
                    dve = P.op("vector", lambda e: e.tensor_copy(
                        out=A["cs"][:], in_=A["pcs"][:, 0:32]), inc="DV")
                    pcs_free[0] = ("DV", dve)
                    for l in range(3):
                        P.op("vector", lambda e, _l=l: e.tensor_reduce(
                            out=A["suff"][:, _l:_l + 1], in_=A["cs"][:, 8 * (_l + 1):32],
                            axis=AX.X, op=Op.add))
                    P.op("vector", lambda e: e.memset(A["suff"][:, 3:4], 0.0), inc="DV")
                    suff_rdy = P.cnt["DV"]
                    jobs = [(l, j) for l in range(NL) for j in range(8 * l + 8)]

                    if True:
                        def emit_score(gi, _h=h, _kh=kh):
                            _l, j = jobs[gi]
                            si = gi % 4
                            rel = j - 8 * _l
                            if psc_free[si] is not None:
                                P.wait("tensor", *psc_free[si])
                            P.wait("tensor", kh["sem"], kh["ready"])
                            pesc = P.op("tensor", lambda e, _j=j, _si=si: e.matmul(
                                out=A["psc"][_si][:],
                                lhsT=_kh["buf"][:, _j * 128:(_j + 1) * 128],
                                rhs=qT[:, _h * TOK + _l * TS:_h * TOK + (_l + 1) * TS],
                                start=True, stop=True), inc="PE")
                            att = A["att"][gi % 4]
                            if rel >= 0:
                                mi = j % 2
                                P.wait("vector", "PE", pesc)
                                P.wait("vector", "IN", mask_rdy)
                                if mtmp_free[mi] is not None:
                                    P.wait("vector", *mtmp_free[mi])
                                dvm = P.op("vector", lambda e, _si=si, _mi=mi, _rel=rel: e.tensor_tensor(
                                    out=A["mtmp"][_mi][:], in0=A["psc"][_si][:],
                                    in1=mask_sb[:, (_l * 8 + _rel) * TS:(_l * 8 + _rel + 1) * TS],
                                    op=Op.mult), inc="DV")
                                psc_free[si] = ("DV", dvm)
                                P.wait("scalar", "DV", dvm)
                                ac = P.op("scalar", lambda e, _mi=mi, _a=att: e.activation(
                                    out=_a[:], in_=A["mtmp"][_mi][:], func=AF.Exp,
                                    scale=INV64), inc="AC")
                                mtmp_free[mi] = ("AC", ac)
                            else:
                                P.wait("scalar", "PE", pesc)
                                ac = P.op("scalar", lambda e, _si=si, _a=att: e.activation(
                                    out=_a[:], in_=A["psc"][_si][:], func=AF.Exp,
                                    scale=INV64), inc="AC")
                                psc_free[si] = ("AC", ac)
                            return ac, att

                        pend = {}
                        LOOK = 4
                        for gi in range(min(LOOK, len(jobs))):
                            pend[gi] = emit_score(gi)
                        for gi, (l, j) in enumerate(jobs):
                            nblk = 8 * l + 8
                            zi, ci = 0, l % 2
                            ac, att = pend.pop(gi)
                            if j == 0 and pz_free[zi] is not None:
                                P.wait("tensor", *pz_free[zi])
                            if j == 0 and pctx_free[ci] is not None:
                                P.wait("tensor", *pctx_free[ci])
                            P.wait("tensor", "AC", ac)
                            P.op("tensor", lambda e, _zi=zi, _a=att, _j=j, _n=nblk: e.matmul(
                                out=A["pz"][_zi][0:1, :], lhsT=ones_bf[:, 0:1], rhs=_a[:],
                                start=(_j == 0), stop=(_j == _n - 1)))
                            P.op("tensor", lambda e, _ci=ci, _a=att, _j=j, _n=nblk, _v=vh: e.matmul(
                                out=A["pctx"][_ci][:],
                                lhsT=_v["buf"][:, _j * 128:(_j + 1) * 128], rhs=_a[:],
                                start=(_j == 0), stop=(_j == _n - 1)),
                                inc="PE")
                            if gi + LOOK < len(jobs):
                                pend[gi + LOOK] = emit_score(gi + LOOK)
                            if j != nblk - 1:
                                continue
                            pe_pv = P.cnt["PE"]
                            count = float(T - nblk * 128)
                            P.wait("vector", "PE", pe_pv)
                            P.op("vector", lambda e, _zi=zi, _c=count: e.tensor_scalar(
                                out=A["zf"][:], in0=A["pz"][_zi][0:1, :], scalar1=_c,
                                scalar2=0.0, op0=Op.add, op1=Op.add))
                            dvz = P.op("vector", lambda e: e.reciprocal(
                                out=A["zr"][:], in_=A["zf"][:]), inc="DV")
                            pz_free[zi] = ("DV", dvz)
                            if pcs_free[0] is not None:
                                P.wait("tensor", *pcs_free[0])
                            P.wait("tensor", "DV", dvz)
                            pezb = P.op("tensor", lambda e: e.matmul(
                                out=A["pcs"][:], lhsT=ones_f[:], rhs=A["zr"][:],
                                start=True, stop=True), inc="PE")
                            P.wait("vector", "PE", pezb)
                            dvzb = P.op("vector", lambda e: e.tensor_copy(
                                out=A["zb"][:], in_=A["pcs"][:]), inc="DV")
                            pcs_free[0] = ("DV", dvzb)
                            cg = cstg.next()
                            if cg["free"] is not None:
                                P.wait("vector", *cg["free"])
                            P.wait("vector", "DV", suff_rdy)
                            dvc = P.op("vector", lambda e, _ci=ci, _l=l, _cg=cg: e.scalar_tensor_tensor(
                                out=_cg["buf"][:], in0=A["pctx"][_ci][:],
                                scalar=A["suff"][:, _l:_l + 1], in1=A["zb"][:],
                                op0=Op.add, op1=Op.mult), inc="DV")
                            pctx_free[ci] = ("DV", dvc)
                            P.wait("gpsimd", "DV", dvc)
                            P.op("gpsimd", lambda e, _l=l, _h=h, _cg=cg: e.dma_start(
                                out=ctx_sw[_l * 128:(_l + 1) * 128, _h * TS:(_h + 1) * TS],
                                in_=_cg["buf"][:]), inc=cg["sem"], n=16)
                            cg["free"] = (cg["sem"], P.cnt[cg["sem"]])
                    kh["free"] = ("PE", P.cnt["PE"])
                    vh["free"] = ("PE", P.cnt["PE"])

            if upto >= 4:
                run_phase(at_alloc, at_emit)

        def wo_alloc(st):
            return {
                "wbuf": sb(st, "wo_wbuf", [128, NCH * D], BF),
                "ctx": [sb(st, f"wo_cx{i}", [128, H * TS], BF) for i in range(2)],
                "xo": [sb(st, f"wo_xo{i}", [128, TS], F32) for i in range(4)],
                "r": sb(st, "wo_r", [128, NCH * TS], F32),
                "rb": [sb(st, f"wo_rb{i}", [128, TS], BF) for i in range(2)],
                "sq": [sb(st, f"wo_sq{i}", [128, TS], BF) for i in range(2)],
                "stat": sb(st, "wo_st", [1, 6 * TS], F32),
                "mb": sb(st, "wo_mb", [128, TS], F32),
                "rstdb": sb(st, "wo_rs", [128, TS], F32),
                "x1f": [sb(st, f"wo_x1f{i}", [128, TS], F32) for i in range(2)],
                "x1b": [sb(st, f"wo_x1b{i}", [128, TS], BF) for i in range(2)],
                "pao": [ps(st, f"wo_pa{i}") for i in range(2)],
                "ps1": ps(st, "wo_ps1"),
                "ps2": ps(st, "wo_ps2"),
                "pb1": ps(st, "wo_pb1"),
                "pb2": ps(st, "wo_pb2"),
            }

        def ln_tail(P, A, eps, state):
            st_ = A["stat"]
            P.op("vector", lambda e: e.tensor_scalar(
                out=st_[:, 0:TS], in0=A["ps1"][0:1, :], scalar1=1.0 / D, scalar2=0.0, op0=Op.mult, op1=Op.add))
            P.op("vector", lambda e: e.tensor_tensor(
                out=st_[:, 2 * TS:3 * TS], in0=st_[:, 0:TS], in1=st_[:, 0:TS],
                op=Op.mult))
            P.op("vector", lambda e: e.scalar_tensor_tensor(
                out=st_[:, 3 * TS:4 * TS], in0=A["ps2"][0:1, :], scalar=1.0 / D,
                in1=st_[:, 2 * TS:3 * TS], op0=Op.mult, op1=Op.subtract))
            dvv = P.op("vector", lambda e, _e=eps: e.tensor_scalar(
                out=st_[:, 4 * TS:5 * TS], in0=st_[:, 3 * TS:4 * TS], scalar1=_e,
                scalar2=0.0, op0=Op.add, op1=Op.add), inc="DV")
            state["ps_free"] = ("DV", dvv)
            P.wait("scalar", "DV", dvv)
            P.op("scalar", lambda e: e.activation(
                out=st_[:, 5 * TS:6 * TS], in_=st_[:, 4 * TS:5 * TS], func=AF.Ln))
            ac = P.op("scalar", lambda e: e.activation(
                out=st_[:, TS:2 * TS], in_=st_[:, 5 * TS:6 * TS], func=AF.Exp,
                scale=-0.5), inc="AC")
            if state.get("pb_free") is not None:
                P.wait("tensor", *state["pb_free"])
            P.wait("tensor", "AC", ac)
            P.op("tensor", lambda e: e.matmul(
                out=A["pb1"][:], lhsT=ones_f[:], rhs=st_[:, 0:TS],
                start=True, stop=True))
            peb = P.op("tensor", lambda e: e.matmul(
                out=A["pb2"][:], lhsT=ones_f[:], rhs=st_[:, TS:2 * TS],
                start=True, stop=True), inc="PE")
            P.wait("vector", "PE", peb)
            P.op("vector", lambda e: e.tensor_copy(out=A["mb"][:], in_=A["pb1"][:]))
            dvb = P.op("vector", lambda e: e.tensor_copy(
                out=A["rstdb"][:], in_=A["pb2"][:]), inc="DV")
            state["pb_free"] = ("DV", dvb)
            return dvb

        def wo_emit(P, A):
            P.wait("sync", "SC", P.cnt["SC"])
            ctx_p = Pool(P, A["ctx"], ["qb0", "qb1"])
            xo_p = Pool(P, A["xo"], ["qc0", "qc1", "qc2", "qc3"])
            x1f_p = Pool(P, A["x1f"], ["w0", "w1"])
            x1b_p = Pool(P, A["x1b"], ["w2", "w3"])
            pao_free = [None, None]
            rb_free = [None, None]
            state = {"ps_free": None, "pb_free": None}
            pend_st = []

            def flush_st(limit):
                while len(pend_st) > limit:
                    acs, ri, first, last = pend_st.pop(0)
                    if first and state["ps_free"] is not None:
                        P.wait("tensor", *state["ps_free"])
                    P.wait("tensor", "AC", acs)
                    P.op("tensor", lambda e, _ri=ri, _f=first, _la=last: e.matmul(
                        out=A["ps1"][0:1, :], lhsT=ones_bf[:, 0:1], rhs=A["rb"][_ri][:],
                        start=_f, stop=_la))
                    pst = P.op("tensor", lambda e, _ri=ri, _f=first, _la=last: e.matmul(
                        out=A["ps2"][0:1, :], lhsT=ones_bf[:, 0:1], rhs=A["sq"][_ri][:],
                        start=_f, stop=_la), inc="PE")
                    rb_free[ri] = ("PE", pst)
            P.sem("qa0")
            P.active_pool_sems.add("qa0")
            P.op("sync", lambda e: e.dma_start(
                out=A["wbuf"][:], in_=wbf_scr[3 * 128:4 * 128, :]), inc="qa0", n=16)
            w_rdy_sem = ("qa0", P.cnt["qa0"])
            for l in range(NL):
                cx = ctx_p.load("sync",
                                lambda _l=l: ctx_sw[_l * 128:(_l + 1) * 128, :])
                for m in range(NCH):
                    pi = m % 2
                    flush_st(1)
                    if pao_free[pi] is not None:
                        P.wait("tensor", *pao_free[pi])
                    P.wait("tensor", cx["sem"], cx["ready"])
                    P.wait("tensor", *w_rdy_sem)
                    for c in range(NCH):
                        P.op("tensor", lambda e, _c=c, _m=m, _pi=pi, _x=cx: e.matmul(
                            out=A["pao"][_pi][:],
                            lhsT=A["wbuf"][:, _c * D + _m * 128:_c * D + _m * 128 + 128],
                            rhs=_x["buf"][:, _c * TS:(_c + 1) * TS],
                            start=(_c == 0), stop=(_c == NCH - 1)),
                            inc="PE" if c == NCH - 1 else None)
                    pev = P.cnt["PE"]
                    xo = xo_p.load("sync",
                                   lambda _l=l, _m=m: xoTf_sw[
                                       (_l * 16 + _m) * 128:(_l * 16 + _m) * 128 + 128, :])
                    P.wait("vector", "PE", pev)
                    P.wait("vector", xo["sem"], xo["ready"])
                    dvr = P.op("vector", lambda e, _pi=pi, _m=m, _xo=xo: e.tensor_tensor(
                        out=A["r"][:, _m * TS:(_m + 1) * TS], in0=A["pao"][_pi][:],
                        in1=_xo["buf"][:], op=Op.add), inc="DV")
                    pao_free[pi] = ("DV", dvr)
                    xo["free"] = ("DV", dvr)
                    ri = m % 2
                    if rb_free[ri] is not None:
                        P.wait("vector", *rb_free[ri])
                    dvrb = P.op("vector", lambda e, _m=m, _ri=ri: e.tensor_copy(
                        out=A["rb"][_ri][:], in_=A["r"][:, _m * TS:(_m + 1) * TS]),
                        inc="DV")
                    P.wait("scalar", "DV", dvrb)
                    acs = P.op("scalar", lambda e, _ri=ri: e.activation(
                        out=A["sq"][_ri][:], in_=A["rb"][_ri][:], func=AF.Square),
                        inc="AC")
                    pend_st.append((acs, ri, m == 0, m == NCH - 1))
                flush_st(0)
                P.wait("vector", "PE", P.cnt["PE"])
                bc_rdy = ln_tail(P, A, EPS_LN, state)
                for m in range(NCH):
                    xf = x1f_p.next()
                    if xf["free"] is not None:
                        P.wait("vector", *xf["free"])
                    xb = x1b_p.next()
                    if xb["free"] is not None:
                        P.wait("vector", *xb["free"])
                    P.wait("vector", "DV", bc_rdy)
                    P.op("vector", lambda e, _m=m, _xf=xf: e.tensor_tensor(
                        out=_xf["buf"][:], in0=A["r"][:, _m * TS:(_m + 1) * TS],
                        in1=A["mb"][:], op=Op.subtract))
                    P.op("vector", lambda e, _xf=xf: e.tensor_tensor(
                        out=_xf["buf"][:], in0=_xf["buf"][:], in1=A["rstdb"][:],
                        op=Op.mult))
                    dvx = P.op("vector", lambda e, _xf=xf, _xb=xb: e.tensor_copy(
                        out=_xb["buf"][:], in_=_xf["buf"][:]), inc="DV")
                    P.wait("gpsimd", "DV", dvx)
                    P.op("gpsimd", lambda e, _l=l, _m=m, _xf=xf: e.dma_start(
                        out=x1f_sw[(_l * 16 + _m) * 128:(_l * 16 + _m) * 128 + 128, :],
                        in_=_xf["buf"][:]), inc=xf["sem"], n=16)
                    xf["free"] = (xf["sem"], P.cnt[xf["sem"]])
                    P.op("gpsimd", lambda e, _l=l, _m=m, _xb=xb: e.dma_start(
                        out=x1b_sw[_l * 128:(_l + 1) * 128, _m * TS:(_m + 1) * TS],
                        in_=_xb["buf"][:]), inc=xb["sem"], n=16)
                    xb["free"] = (xb["sem"], P.cnt[xb["sem"]])
                cx["free"] = ("PE", P.cnt["PE"])

        if upto >= 5:
            run_phase(wo_alloc, wo_emit)

        for l in range(NL):
            if upto < 6:
                break
            with ExitStack() as sl:
                h_strip = sb(sl, f"f_h{l}", [128, 64 * TS], BF)
                invb = sb(sl, f"f_inv{l}", [128, TS], F32)

                def fp_alloc(st, _l=l):
                    return {
                        "x1s": [sb(st, f"f_x1_{_l}", [128, 16 * TS], BF)],
                        "wg": [sb(st, f"f_wg{i}_{_l}", [128, 16 * TS], BF) for i in range(2)],
                        "wu": [sb(st, f"f_wu{i}_{_l}", [128, 16 * TS], BF) for i in range(2)],
                        "sil": [sb(st, f"f_sil{i}_{_l}", [128, TS], F32) for i in range(2)],
                        "sq": [sb(st, f"f_sq{i}_{_l}", [128, TS], BF) for i in range(4)],
                        "s2": sb(st, f"f_s2_{_l}", [1, 3 * TS], F32),
                        "pg": [ps(st, f"f_pg{i}_{_l}") for i in range(2)],
                        "pu": [ps(st, f"f_pu{i}_{_l}") for i in range(2)],
                        "phs": ps(st, f"f_phs_{_l}"),
                        "pib": ps(st, f"f_pib_{_l}"),
                    }

                def fp_emit(P, A, _l=l):
                    P.wait("sync", "SC", P.cnt["SC"])
                    x1_p = Pool(P, A["x1s"], ["qa0"])
                    wg_p = Pool(P, A["wg"], ["qb0", "qb1"])
                    wu_p = Pool(P, A["wu"], ["qc0", "qc1"])
                    pg_free = [None, None]
                    pu_free = [None, None]
                    sil_free = [None, None]
                    sq_free = [None, None, None, None]
                    pend_sq = []

                    def flush_sq(limit):
                        while len(pend_sq) > limit:
                            acq, qi, first, last = pend_sq.pop(0)
                            P.wait("tensor", "AC", acq)
                            pq = P.op("tensor", lambda e, _qi=qi, _f=first, _la=last: e.matmul(
                                out=A["phs"][0:1, :], lhsT=ones_bf[:, 0:1],
                                rhs=A["sq"][_qi][:],
                                start=_f, stop=_la), inc="PE")
                            sq_free[qi] = ("PE", pq)
                    P.wait("sync", "CC", 3)
                    x1 = x1_p.load("sync",
                                   lambda: x1b_sw[_l * 128:(_l + 1) * 128, :])
                    for g in range(16):
                        wg = wg_p.load("sync",
                                       lambda _g=g: gath_w[wgu_row(_g):wgu_row(_g) + 128, :])
                        wu = wu_p.load("sync",
                                       lambda _g=g: gath_w[wgu_row(16 + _g):wgu_row(16 + _g) + 128, :])
                        for k in range(4):
                            pi = k % 2
                            flush_sq(2)
                            if pg_free[pi] is not None:
                                P.wait("tensor", *pg_free[pi])
                            if pu_free[pi] is not None:
                                P.wait("tensor", *pu_free[pi])
                            P.wait("tensor", x1["sem"], x1["ready"])
                            P.wait("tensor", wg["sem"], wg["ready"])
                            P.wait("tensor", wu["sem"], wu["ready"])
                            for dc in range(NCH):
                                P.op("tensor", lambda e, _dc=dc, _k=k, _pi=pi, _w=wg, _x=x1: e.matmul(
                                    out=A["pg"][_pi][:],
                                    lhsT=_w["buf"][:, _dc * TS + _k * 128:_dc * TS + _k * 128 + 128],
                                    rhs=_x["buf"][:, _dc * TS:(_dc + 1) * TS],
                                    start=(_dc == 0), stop=(_dc == NCH - 1)),
                                    inc="PE" if dc == NCH - 1 else None)
                            peg = P.cnt["PE"]
                            for dc in range(NCH):
                                P.op("tensor", lambda e, _dc=dc, _k=k, _pi=pi, _w=wu, _x=x1: e.matmul(
                                    out=A["pu"][_pi][:],
                                    lhsT=_w["buf"][:, _dc * TS + _k * 128:_dc * TS + _k * 128 + 128],
                                    rhs=_x["buf"][:, _dc * TS:(_dc + 1) * TS],
                                    start=(_dc == 0), stop=(_dc == NCH - 1)),
                                    inc="PE" if dc == NCH - 1 else None)
                            peu = P.cnt["PE"]
                            P.wait("scalar", "PE", peg)
                            if sil_free[pi] is not None:
                                P.wait("scalar", *sil_free[pi])
                            acg = P.op("scalar", lambda e, _pi=pi: e.activation(
                                out=A["sil"][_pi][:], in_=A["pg"][_pi][:], func=AF.Silu),
                                inc="AC")
                            pg_free[pi] = ("AC", acg)
                            P.wait("vector", "AC", acg)
                            P.wait("vector", "PE", peu)
                            hcol = (g * 4 + k) * TS
                            dvh = P.op("vector", lambda e, _pi=pi, _hc=hcol: e.tensor_tensor(
                                out=h_strip[:, _hc:_hc + TS], in0=A["sil"][_pi][:],
                                in1=A["pu"][_pi][:], op=Op.mult), inc="DV")
                            pu_free[pi] = ("DV", dvh)
                            sil_free[pi] = ("DV", dvh)
                            P.wait("scalar", "DV", dvh)
                            qi = (g * 4 + k) % 4
                            if sq_free[qi] is not None:
                                P.wait("scalar", *sq_free[qi])
                            acq = P.op("scalar", lambda e, _qi=qi, _hc=hcol: e.activation(
                                out=A["sq"][_qi][:], in_=h_strip[:, _hc:_hc + TS],
                                func=AF.Square), inc="AC")
                            pend_sq.append((acq, qi, g == 0 and k == 0,
                                            g == 15 and k == 3))
                        wg["free"] = ("PE", P.cnt["PE"])
                        wu["free"] = ("PE", P.cnt["PE"])
                    flush_sq(0)
                    x1["free"] = ("PE", P.cnt["PE"])
                    P.wait("vector", "PE", P.cnt["PE"])
                    dvs = P.op("vector", lambda e: e.tensor_scalar(
                        out=A["s2"][:, 0:TS], in0=A["phs"][0:1, :], scalar1=1.0 / HF,
                        scalar2=EPS_RMS, op0=Op.mult, op1=Op.add), inc="DV")
                    P.wait("scalar", "DV", dvs)
                    P.op("scalar", lambda e: e.activation(
                        out=A["s2"][:, TS:2 * TS], in_=A["s2"][:, 0:TS], func=AF.Ln))
                    aci = P.op("scalar", lambda e: e.activation(
                        out=A["s2"][:, 2 * TS:3 * TS], in_=A["s2"][:, TS:2 * TS],
                        func=AF.Exp, scale=-0.5), inc="AC")
                    P.wait("tensor", "AC", aci)
                    pei = P.op("tensor", lambda e: e.matmul(
                        out=A["pib"][:], lhsT=ones_f[:], rhs=A["s2"][:, 2 * TS:3 * TS],
                        start=True, stop=True), inc="PE")
                    P.wait("vector", "PE", pei)
                    P.op("vector", lambda e: e.tensor_copy(out=invb[:], in_=A["pib"][:]),
                         inc="DV")

                run_phase(fp_alloc, fp_emit)
                h_rdy = P.cnt["DV"]

                def pj_alloc(st, _l=l):
                    return {
                        "wpm": [sb(st, f"f_wp{i}_{_l}", [128, 64 * 128], BF) for i in range(2)],
                        "x1f": [sb(st, f"f_x1f{i}_{_l}", [128, TS], F32) for i in range(2)],
                        "r2": sb(st, f"f_r2_{_l}", [128, NCH * TS], F32),
                        "rb": [sb(st, f"f_rb{i}_{_l}", [128, TS], BF) for i in range(2)],
                        "sq": [sb(st, f"f_sq2{i}_{_l}", [128, TS], BF) for i in range(2)],
                        "stat": sb(st, f"f_st_{_l}", [1, 6 * TS], F32),
                        "mb": sb(st, f"f_mb_{_l}", [128, TS], F32),
                        "rstdb": sb(st, f"f_rs_{_l}", [128, TS], F32),
                        "onv": [sb(st, f"f_on{i}_{_l}", [128, TS], F32) for i in range(2)],
                        "ostg": [sb(st, f"f_og{i}_{_l}", [128, D], F32) for i in range(4)],
                        "pp": [ps(st, f"f_pp{i}_{_l}") for i in range(2)],
                        "ps1": ps(st, f"f_ps1_{_l}"),
                        "ps2": ps(st, f"f_ps2_{_l}"),
                        "pb1": ps(st, f"f_pb1_{_l}"),
                        "pb2": ps(st, f"f_pb2_{_l}"),
                        "pot": [ps(st, f"f_pt{i}_{_l}") for i in range(2)],
                    }

                def pj_emit(P, A, _l=l, _h_rdy=h_rdy):
                    P.wait("sync", "SC", P.cnt["SC"])
                    wpm_p = Pool(P, A["wpm"], ["qa0", "qa1"])
                    x1f_p = Pool(P, A["x1f"], ["qb0", "qb1"])
                    ostg = Pool(P, A["ostg"], ["w0", "w1", "w2", "w3"])
                    pp_free = [None, None]
                    rb_free = [None, None]
                    onv_free = [None, None]
                    pot_free = [None, None]
                    state = {"ps_free": None, "pb_free": None}
                    pend_st = []

                    def flush_st(limit):
                        while len(pend_st) > limit:
                            acs, ri, first, last = pend_st.pop(0)
                            if first and state["ps_free"] is not None:
                                P.wait("tensor", *state["ps_free"])
                            P.wait("tensor", "AC", acs)
                            P.op("tensor", lambda e, _ri=ri, _f=first, _la=last: e.matmul(
                                out=A["ps1"][0:1, :], lhsT=ones_bf[:, 0:1],
                                rhs=A["rb"][_ri][:], start=_f, stop=_la))
                            pst = P.op("tensor", lambda e, _ri=ri, _f=first, _la=last: e.matmul(
                                out=A["ps2"][0:1, :], lhsT=ones_bf[:, 0:1],
                                rhs=A["sq"][_ri][:], start=_f, stop=_la), inc="PE")
                            rb_free[ri] = ("PE", pst)
                    for m in range(NCH):
                        P.wait("sync", "CC", 3)
                        wp = wpm_p.load("sync",
                                        lambda _m=m: gath_w[wp_row(_m):wp_row(_m) + 128, :])
                        pi = m % 2
                        flush_st(1)
                        if pp_free[pi] is not None:
                            P.wait("tensor", *pp_free[pi])
                        P.wait("tensor", wp["sem"], wp["ready"])
                        P.wait("tensor", "DV", _h_rdy)
                        for hc in range(64):
                            P.op("tensor", lambda e, _hc=hc, _pi=pi, _w=wp: e.matmul(
                                out=A["pp"][_pi][:],
                                lhsT=_w["buf"][:, _hc * 128:(_hc + 1) * 128],
                                rhs=h_strip[:, _hc * TS:(_hc + 1) * TS],
                                start=(_hc == 0), stop=(_hc == 63)),
                                inc="PE" if hc == 63 else None)
                        pev = P.cnt["PE"]
                        wp["free"] = ("PE", pev)
                        xf = x1f_p.load("sync",
                                        lambda _m=m: x1f_sw[
                                            (_l * 16 + _m) * 128:(_l * 16 + _m) * 128 + 128, :])
                        P.wait("vector", "PE", pev)
                        P.wait("vector", xf["sem"], xf["ready"])
                        P.op("vector", lambda e, _pi=pi, _m=m: e.tensor_tensor(
                            out=A["r2"][:, _m * TS:(_m + 1) * TS], in0=A["pp"][_pi][:],
                            in1=invb[:], op=Op.mult))
                        dvr = P.op("vector", lambda e, _m=m, _xf=xf: e.tensor_tensor(
                            out=A["r2"][:, _m * TS:(_m + 1) * TS],
                            in0=A["r2"][:, _m * TS:(_m + 1) * TS],
                            in1=_xf["buf"][:], op=Op.add), inc="DV")
                        pp_free[pi] = ("DV", dvr)
                        xf["free"] = ("DV", dvr)
                        ri = m % 2
                        if rb_free[ri] is not None:
                            P.wait("vector", *rb_free[ri])
                        dvrb = P.op("vector", lambda e, _m=m, _ri=ri: e.tensor_copy(
                            out=A["rb"][_ri][:], in_=A["r2"][:, _m * TS:(_m + 1) * TS]),
                            inc="DV")
                        P.wait("scalar", "DV", dvrb)
                        acs = P.op("scalar", lambda e, _ri=ri: e.activation(
                            out=A["sq"][_ri][:], in_=A["rb"][_ri][:], func=AF.Square),
                            inc="AC")
                        pend_st.append((acs, ri, m == 0, m == NCH - 1))
                    flush_st(0)
                    P.wait("vector", "PE", P.cnt["PE"])
                    bc_rdy = ln_tail(P, A, EPS_LN, state)
                    ogs = [ostg.next() for _ in range(4)]
                    for og in ogs:
                        if og["free"] is not None:
                            P.wait("vector", *og["free"])
                    for m in range(NCH):
                        oi = m % 2
                        P.wait("vector", "DV", bc_rdy)
                        if onv_free[oi] is not None:
                            P.wait("vector", *onv_free[oi])
                        P.op("vector", lambda e, _m=m, _oi=oi: e.tensor_tensor(
                            out=A["onv"][_oi][:], in0=A["r2"][:, _m * TS:(_m + 1) * TS],
                            in1=A["mb"][:], op=Op.subtract))
                        dvo = P.op("vector", lambda e, _oi=oi: e.tensor_tensor(
                            out=A["onv"][_oi][:], in0=A["onv"][_oi][:], in1=A["rstdb"][:],
                            op=Op.mult), inc="DV")
                        for stt in range(4):
                            ti = (m * 4 + stt) % 2
                            if pot_free[ti] is not None:
                                P.wait("tensor", *pot_free[ti])
                            P.wait("tensor", "DV", dvo)
                            pet = P.op("tensor", lambda e, _oi=oi, _st=stt, _ti=ti: e.transpose(
                                out=A["pot"][_ti][:, 0:128],
                                in_=A["onv"][_oi][:, _st * 128:(_st + 1) * 128],
                                identity=id_f[:]), inc="PE")
                            P.wait("vector", "PE", pet)
                            dve = P.op("vector", lambda e, _ti=ti, _st=stt, _m=m: e.tensor_copy(
                                out=ogs[_st]["buf"][:, _m * 128:(_m + 1) * 128],
                                in_=A["pot"][_ti][:, 0:128]), inc="DV")
                            pot_free[ti] = ("DV", dve)
                        onv_free[oi] = ("PE", P.cnt["PE"])
                    P.wait("gpsimd", "DV")
                    for stt in range(4):
                        P.op("gpsimd", lambda e, _st=stt: e.dma_start(
                            out=out[_l * TS + _st * 128:_l * TS + (_st + 1) * 128, :],
                            in_=ogs[_st]["buf"][:]), inc=ogs[stt]["sem"], n=16)
                        ogs[stt]["free"] = (ogs[stt]["sem"], P.cnt[ogs[stt]["sem"]])

                run_phase(pj_alloc, pj_emit)

        def fin_emit(P, A):
            P.wait("gpsimd", "SC", P.cnt["SC"])
            P.wait("gpsimd", "IN", P.cnt["IN"])

        run_phase(lambda st: {}, fin_emit)

    return nc


def make_in_maps(x, Wq, Wk, Wv, Wo, wgu, wp):
    wgu_shards = [np.ascontiguousarray(np.concatenate(
        [wgu[(c + 8 * k) * 512:(c + 8 * k + 1) * 512] for k in range(4)], axis=0))
        for c in range(8)]
    wp_shards = [np.ascontiguousarray(np.concatenate(
        [wp[(c + 8 * k) * 128:(c + 8 * k + 1) * 128] for k in range(2)], axis=0))
        for c in range(8)]
    in_maps = []
    ones_bf = np.ones((128, 128), dtype=NPBF)
    ones_f = np.ones((1, 128), dtype=np.float32)
    id_bf = np.eye(128, dtype=np.float32).astype(NPBF)
    id_f = np.eye(128, dtype=np.float32)
    pp = np.arange(128)[:, None]
    tt = np.arange(TS)[None, :]
    for c in range(8):
        b, half = c // 2, c % 2
        xb = np.ascontiguousarray(x[b])
        x_own = np.ascontiguousarray(
            np.concatenate([xb[(2 * l + half) * TS:(2 * l + half + 1) * TS]
                            for l in range(NL)], axis=0))
        mb = np.zeros((128, NL * 8 * TS), dtype=NPBF)
        for l in range(NL):
            E = 2 * l + half
            for rel in range(8):
                j = 8 * l + rel
                keep = (j * 128 + pp) <= (E * TS + tt)
                mb[:, (l * 8 + rel) * TS:(l * 8 + rel + 1) * TS] = keep.astype(NPBF)
        in_maps.append({
            "x_own": x_own,
            "wq": Wq, "wk": Wk, "wv": Wv, "wo": Wo,
            "wgu_my": wgu_shards[c], "wp_my": wp_shards[c],
            "c_mask": mb,
            "c_ones_bf": ones_bf,
            "c_ones_f": ones_f,
            "c_id_bf": id_bf,
            "c_id_f": id_f,
        })
    return in_maps


def kernel(L0_attn_x, L0_attn_Wq, L0_attn_Wk, L0_attn_Wv, L0_attn_Wo,
           L0_ffn_w_gate_up, L0_ffn_w_proj):
    x = np.ascontiguousarray(L0_attn_x, dtype=np.float32)
    nc = build()
    in_maps = make_in_maps(
        x,
        np.ascontiguousarray(L0_attn_Wq, dtype=np.float32),
        np.ascontiguousarray(L0_attn_Wk, dtype=np.float32),
        np.ascontiguousarray(L0_attn_Wv, dtype=np.float32),
        np.ascontiguousarray(L0_attn_Wo, dtype=np.float32),
        np.ascontiguousarray(L0_ffn_w_gate_up, dtype=np.float32),
        np.ascontiguousarray(L0_ffn_w_proj, dtype=np.float32))
    res = run_bass_kernel_spmd(nc, in_maps, core_ids=list(range(8)))
    outf = np.empty((B, T, D), dtype=np.float32)
    for c in range(8):
        b, half = c // 2, c % 2
        oc = res.results[c]["out"]
        for l in range(NL):
            E = 2 * l + half
            outf[b, E * TS:(E + 1) * TS] = oc[l * TS:(l + 1) * TS]
    return outf


# version 18
# speedup vs baseline: 1.0261x; 1.0016x over previous
import numpy as np
import ml_dtypes
from contextlib import ExitStack

import concourse.bass as bass
import concourse.mybir as mybir
from concourse.bass_utils import run_bass_kernel_spmd
from concourse.alu_op_type import AluOpType as Op

BF = mybir.dt.bfloat16
F32 = mybir.dt.float32
AF = mybir.ActivationFunctionType
AX = mybir.AxisListType

C_THR = 0.5009765625
B, T, D, H, DH = 4, 4096, 2048, 16, 128
TS = 512
NL = 4
TOK = NL * TS
NCH = D // 128
HF = 8192
EPS_LN = 1e-5
EPS_RMS = 1.1920929e-7
INV64 = 1.0 / 64.0
ENGINES = ("sync", "vector", "scalar", "tensor", "gpsimd")

NPBF = ml_dtypes.bfloat16


class Prog:

    def __init__(self, nc, stack):
        self.nc = nc
        self.stack = stack
        self.cur = None
        self.cnt = {}
        self.sems = {}
        self.active_pool_sems = set()

    def sem(self, name):
        if name not in self.sems:
            self.sems[name] = self.stack.enter_context(self.nc.semaphore(name))
        if name not in self.cnt:
            self.cnt[name] = 0
        return self.sems[name]

    AUTOFENCE = {"vector": "DV", "scalar": "AC"}

    def op(self, engname, fn, inc=None, n=1, nofence=False):
        auto = self.AUTOFENCE.get(engname)
        if auto is not None and not nofence:
            self.wait(engname, auto)
            if inc is None:
                inc = auto
        if inc is not None:
            self.sem(inc)
        if self.cur == engname:
            instr = fn(getattr(self.nc, engname))
            if inc is not None:
                instr.then_inc(self.sems[inc], n)
        if inc is not None:
            self.cnt[inc] += n
            return self.cnt[inc]
        return None

    def wait(self, engname, semname, val=None):
        self.sem(semname)
        v = self.cnt[semname] if val is None else val
        if v <= 0:
            return
        if self.cur == engname:
            getattr(self.nc, engname).wait_ge(self.sems[semname], v)

    def waitl(self, engname, items):
        for it in items:
            if it is not None:
                self.wait(engname, it[0], it[1])

    def drain_pools(self, engname):
        for sem in sorted(self.active_pool_sems):
            self.wait(engname, sem, self.cnt[sem])
        self.active_pool_sems = set()


class Pool:

    def __init__(self, P, bufs, semnames):
        self.P = P
        self.slots = [{"buf": b, "sem": s, "ready": 0, "free": None}
                      for b, s in zip(bufs, semnames)]
        for s in self.slots:
            if s["sem"] is not None:
                P.sem(s["sem"])
                P.active_pool_sems.add(s["sem"])
        self.i = 0

    def next(self):
        s = self.slots[self.i % len(self.slots)]
        self.i += 1
        return s

    def load(self, q, src_fn, waitl=()):
        s = self.next()
        if s["free"] is not None:
            self.P.waitl(q, s["free"] if isinstance(s["free"], list) else [s["free"]])
        for w in waitl:
            self.P.wait(q, *w)
        self.P.op(q, lambda e: e.dma_start(out=s["buf"][:], in_=src_fn()),
                  inc=s["sem"], n=16)
        s["ready"] = self.P.cnt[s["sem"]]
        return s


def mid3(ap2, mid_step, mid_count):
    return bass.AP(ap2.tensor, ap2.offset, [ap2.ap[0], [mid_step, mid_count], ap2.ap[1]])


def bview(ap2, bcast):
    return bass.AP(ap2.tensor, ap2.offset, list(ap2.ap) + [[0, bcast]])


def build(taps=(), upto=99):
    nc = bass.Bass("TRN2", target_bir_lowering=False)

    def dram(name, shape, dt, kind=None):
        if name in taps:
            kind = "ExternalOutput"
        if kind is None:
            return nc.dram_tensor(name, shape, dt).ap()
        return nc.dram_tensor(name, shape, dt, kind=kind).ap()

    x_own = dram("x_own", [TOK, D], F32, "ExternalInput")
    wq_in = dram("wq", [D, D], F32, "ExternalInput")
    wk_in = dram("wk", [D, D], F32, "ExternalInput")
    wv_in = dram("wv", [D, D], F32, "ExternalInput")
    wo_in = dram("wo", [D, D], F32, "ExternalInput")
    wgu_my = dram("wgu_my", [4 * 512, D], F32, "ExternalInput")
    wp_my = dram("wp_my", [2 * 128, HF], F32, "ExternalInput")
    c_mask = dram("c_mask", [128, NL * 8 * TS], BF, "ExternalInput")
    c_ones_bf = dram("c_ones_bf", [128, 128], BF, "ExternalInput")
    c_ones_f = dram("c_ones_f", [1, 128], F32, "ExternalInput")
    c_id_bf = dram("c_id_bf", [128, 128], BF, "ExternalInput")
    c_id_f = dram("c_id_f", [128, 128], F32, "ExternalInput")
    out = dram("out", [TOK, D], F32, "ExternalOutput")

    wbf_scr = dram("wbf_scr", [4 * 128, NCH * D], BF)
    contrib_w = dram("contrib_w", [6 * 128, 16 * TS], BF)
    gath_w = nc.dram_tensor("gath_w", [8 * 6 * 128, 16 * TS], BF,
                            addr_space="Shared").ap()

    def wgu_row(g):
        return ((g % 8) * 6 + g // 8) * 128

    def wp_row(m):
        return ((m % 8) * 6 + 4 + m // 8) * 128
    xoT_sw = dram("xoT_sw", [NL * 128, 16 * TS], BF)
    xoTf_sw = dram("xoTf_sw", [NL * 16 * 128, TS], F32)
    kc_scr = dram("kc_scr", [D, TOK], BF)
    vc_scr = dram("vc_scr", [H * 128, TOK], BF)
    kg_scr = dram("kg_scr", [2 * D, TOK], BF)
    vg_scr = dram("vg_scr", [2 * H * 128, TOK], BF)
    ctx_sw = dram("ctx_sw", [NL * 128, H * TS], BF)
    x1f_sw = dram("x1f_sw", [NL * 16 * 128, TS], F32)
    x1b_sw = dram("x1b_sw", [NL * 128, 16 * TS], BF)

    gstack = ExitStack()
    with gstack:
        P = Prog(nc, gstack)
        for s in ("IN", "SC", "PE", "DV", "AC"):
            P.sem(s)

        def sb(st, name, shape, dt, side="left"):
            return st.enter_context(nc.sbuf_tensor(name, shape, dt, side=side))

        def ps(st, name, dt=F32):
            cols = TS if dt == F32 else 1024
            return st.enter_context(nc.psum_tensor(name, [128, cols], dt))

        def run_phase(allocf, emitf):
            with ExitStack() as st:
                bufs = allocf(st)
                with nc.Block() as blk:
                    snap = dict(P.cnt)
                    for en in ENGINES:
                        P.cnt = dict(snap)
                        P.cur = en
                        P.active_pool_sems = set()

                        def body(eng, _f=emitf, _b=bufs):
                            _f(P, _b)
                            P.drain_pools("gpsimd")
                        getattr(blk, en)(body)
                    P.cur = None

        ones_bf = sb(gstack, "ones_bf", [128, 128], BF, side="right")
        ones_f = sb(gstack, "ones_f", [1, 128], F32, side="right")
        id_bf = sb(gstack, "id_bf", [128, 128], BF, side="right")
        id_f = sb(gstack, "id_f", [128, 128], F32, side="right")

        def ph_consts(P, A):
            for dst, src in ((ones_bf, c_ones_bf), (ones_f, c_ones_f),
                             (id_bf, c_id_bf), (id_f, c_id_f)):
                P.op("sync", lambda e, d=dst, s=src: e.dma_start(out=d[:], in_=s),
                     inc="IN", n=16)
            P.wait("sync", "IN")

        run_phase(lambda st: {}, ph_consts)

        def q_alloc(st):
            return {
                "wfin": [sb(st, f"q_wf{i}", [128, D], F32) for i in range(2)],
                "wcst": [sb(st, f"q_wc{i}", [128, D], BF) for i in range(2)],
                "win": [sb(st, f"q_win{i}", [128, 4 * D], F32) for i in range(2)],
                "wb": sb(st, "q_wb", [128, D], BF),
                "t1": sb(st, "q_t1", [128, D], BF),
                "t2": sb(st, "q_t2", [128, D], BF),
                "sums": sb(st, "q_sums", [128, D // 64], F32),
                "sbf": sb(st, "q_sbf", [128, D // 64], BF),
                "thr": sb(st, "q_thr", [128, D // 64], F32),
                "wqt": [sb(st, f"q_wqt{i}", [128, D], BF) for i in range(2)],
                "stage": [sb(st, f"q_stg{i}", [128, 16 * TS], BF) for i in range(2)],
                "psq": [ps(st, f"q_ps{i}", BF) for i in range(2)],
            }

        def q_emit(P, A):
            win = Pool(P, A["win"], ["qa0", "qa1"])
            stage = Pool(P, A["stage"], ["w0", "w1"])
            wfin = Pool(P, A["wfin"], ["qd0", "qd1"])
            wcst = Pool(P, A["wcst"], ["w6", "w7"])
            wjobs = [(wi, c) for wi in range(4) for c in range(NCH)]
            wstate = [0]

            def pump_w(n):
                for _ in range(n):
                    if wstate[0] >= len(wjobs):
                        return
                    wi, c = wjobs[wstate[0]]
                    wstate[0] += 1
                    wt = (wk_in, wv_in, wq_in, wo_in)[wi]
                    s = wfin.load("sync", lambda _c=c, _w=wt: _w[_c * 128:(_c + 1) * 128, :])
                    dstc = wcst.next()
                    if dstc["free"] is not None:
                        P.wait("scalar", *dstc["free"])
                    P.wait("scalar", s["sem"], s["ready"])
                    acw = P.op("scalar", lambda e, _s=s, _d=dstc: e.copy(
                        out=_d["buf"][:], in_=_s["buf"][:]), inc="AC", nofence=True)
                    s["free"] = ("AC", acw)
                    P.wait("gpsimd", "AC", acw)
                    P.op("gpsimd", lambda e, _wi=wi, _c=c, _d=dstc: e.dma_start(
                        out=wbf_scr[_wi * 128:(_wi + 1) * 128, _c * D:(_c + 1) * D],
                        in_=_d["buf"][:]), inc=dstc["sem"], n=16)
                    dstc["free"] = (dstc["sem"], P.cnt[dstc["sem"]])
            psq_free = [None, None]
            wqt_free = [None, None]
            groups = [("wgu", g) for g in range(4)] + [("wp", m) for m in range(2)]
            for gi, (kind, g) in enumerate(groups):
                if kind == "wgu":
                    src = lambda _k=g: bass.AP(wgu_my.tensor, _k * 4 * 128 * D,
                                               [[D, 128], [128 * D, 4], [1, D]])
                else:
                    src = lambda _k=g: bass.AP(wp_my.tensor, _k * 128 * HF,
                                               [[HF, 128], [D, 4], [1, D]])
                ws = win.load("sync", src)
                pump_w(11)
                stg = stage.next()
                if stg["free"] is not None:
                    P.wait("vector", *stg["free"])
                for sub in range(4):
                    P.wait("vector", ws["sem"], ws["ready"])
                    wsub = lambda _s=sub, _b=ws["buf"]: _b[:, _s * D:(_s + 1) * D]
                    P.op("vector", lambda e: e.tensor_copy(out=A["wb"][:], in_=wsub()))
                    P.op("vector", lambda e: e.tensor_reduce(
                        out=A["sums"][:],
                        in_=A["wb"][:].rearrange("p (g k) -> p g k", k=64),
                        axis=AX.X, op=Op.add, apply_absolute_value=True))
                    P.op("vector", lambda e: e.tensor_scalar(
                        out=A["sbf"][:], in0=A["sums"][:], scalar1=INV64,
                        scalar2=1e-8, op0=Op.mult, op1=Op.max))
                    P.op("vector", lambda e: e.tensor_scalar(
                        out=A["thr"][:], in0=A["sbf"][:], scalar1=C_THR, scalar2=0.0, op0=Op.mult, op1=Op.add))
                    thr_b = bview(A["thr"][:], 64)
                    sbf_b = bview(A["sbf"][:], 64)
                    w3 = A["wb"][:].rearrange("p (g k) -> p g k", k=64)
                    t13 = A["t1"][:].rearrange("p (g k) -> p g k", k=64)
                    t23 = A["t2"][:].rearrange("p (g k) -> p g k", k=64)
                    wi = (gi * 4 + sub) % 2
                    wqt = A["wqt"][wi]
                    if wqt_free[wi] is not None:
                        P.wait("vector", *wqt_free[wi])
                    P.op("vector", lambda e: e.tensor_tensor(
                        out=t13, in0=w3, in1=thr_b, op=Op.is_gt))
                    P.op("vector", lambda e: e.scalar_tensor_tensor(
                        out=t23, in0=w3, scalar=-1.0, in1=thr_b,
                        op0=Op.mult, op1=Op.is_gt))
                    P.op("vector", lambda e: e.tensor_tensor(
                        out=A["t1"][:], in0=A["t1"][:], in1=A["t2"][:], op=Op.subtract))
                    P.op("vector", lambda e: e.tensor_tensor(
                        out=t13, in0=t13, in1=sbf_b, op=Op.mult))
                    P.op("vector", lambda e: e.tensor_tensor(
                        out=A["t1"][:], in0=A["t1"][:], in1=A["wb"][:], op=Op.subtract))
                    dvv = P.op("vector", lambda e, _w=wqt: e.tensor_tensor(
                        out=_w[:], in0=A["wb"][:], in1=A["t1"][:], op=Op.add), inc="DV")
                    for k4 in range(4):
                        pi = k4 % 2
                        if psq_free[pi] is not None:
                            P.wait("tensor", *psq_free[pi])
                        P.wait("tensor", "DV", dvv)
                        for kk in range(4):
                            k = k4 * 4 + kk
                            pe = P.op("tensor", lambda e, _k=k, _pi=pi, _kk=kk, _w=wqt: e.transpose(
                                out=A["psq"][_pi][:, _kk * 128:(_kk + 1) * 128],
                                in_=_w[:, _k * 128:(_k + 1) * 128],
                                identity=id_bf[:]), inc="PE")
                        P.wait("vector", "PE", pe)
                        if kind == "wgu":
                            dst_ap = bass.AP(stg["buf"][:].tensor,
                                             stg["buf"][:].offset + (4 * k4 * TS + sub * 128),
                                             [stg["buf"][:].ap[0], [TS, 4], [1, 128]])
                            src_ap = A["psq"][pi][:, 0:TS].rearrange(
                                "p (a b) -> p a b", b=128)
                        else:
                            dst_ap = stg["buf"][:, sub * 2048 + k4 * TS:
                                                sub * 2048 + (k4 + 1) * TS]
                            src_ap = A["psq"][pi][:, 0:TS]
                        dve = P.op("vector", lambda e, _d=dst_ap, _s=src_ap: e.tensor_copy(
                            out=_d, in_=_s), inc="DV")
                        psq_free[pi] = ("DV", dve)
                ws["free"] = ("DV", P.cnt["DV"])
                wqt_free[wi] = ("PE", P.cnt["PE"])
                P.wait("gpsimd", "DV")
                dst = lambda _gi=gi: contrib_w[_gi * 128:(_gi + 1) * 128, :]
                P.op("gpsimd", lambda e, _sg=stg: e.dma_start(out=dst(), in_=_sg["buf"][:]),
                     inc=stg["sem"], n=16)
                stg["free"] = (stg["sem"], P.cnt[stg["sem"]])
            pump_w(64)

        if upto >= 1:
            run_phase(q_alloc, q_emit)

        def xt_alloc(st):
            return {
                "xin": [sb(st, f"x_in{i}", [128, 4 * D], F32) for i in range(2)],
                "xb": sb(st, "x_xb", [128, 4 * D], BF),
                "stg_b": [sb(st, f"x_sb{i}", [128, 16 * TS], BF) for i in range(2)],
                "stg_f": [sb(st, "x_sf0", [128, 16 * TS], F32)],
                "psb": [ps(st, f"x_pb{i}", BF) for i in range(2)],
                "psf": [ps(st, f"x_pf{i}", F32) for i in range(2)],
            }

        def xt_emit(P, A):
            P.wait("sync", "SC", P.cnt["SC"])
            xin = Pool(P, A["xin"], ["qa0", "qa1"])
            stgb_p = Pool(P, A["stg_b"], ["w0", "w1"])
            stgf_p = Pool(P, A["stg_f"], ["w2"])
            psb_free = [None, None]
            psf_free = [None, None]
            jobs = [("own", s) for s in range(NL)]
            for kind, sec in jobs:
                src_t = x_own.tensor
                xs = xin.load("sync",
                              lambda _s=sec, _t=src_t: bass.AP(
                                  _t, _s * TS * D, [[D, 128], [128 * D, 4], [1, D]]))
                P.wait("vector", xs["sem"], xs["ready"])
                dvc = P.op("vector", lambda e, _x=xs: e.tensor_copy(
                    out=A["xb"][:], in_=_x["buf"][:]), inc="DV")
                stgb = stgb_p.next()
                if stgb["free"] is not None:
                    P.wait("vector", *stgb["free"])
                stgf = None
                if kind == "own":
                    stgf = stgf_p.next()
                    if stgf["free"] is not None:
                        P.wait("vector", *stgf["free"])
                for dc in range(NCH):
                    pi = dc % 2
                    if psb_free[pi] is not None:
                        P.wait("tensor", *psb_free[pi])
                    P.wait("tensor", "DV", dvc)
                    for tile in range(4):
                        pe = P.op("tensor", lambda e, _c=tile * D + dc * 128, _p=pi,
                                  _t=tile: e.transpose(
                            out=A["psb"][_p][:, _t * 128:(_t + 1) * 128],
                            in_=A["xb"][:, _c:_c + 128],
                            identity=id_bf[:]), inc="PE")
                    P.wait("vector", "PE", pe)
                    dve = P.op("vector", lambda e, _p=pi, _d=dc * TS, _sg=stgb: e.tensor_copy(
                        out=_sg["buf"][:, _d:_d + TS], in_=A["psb"][_p][:, 0:TS]),
                        inc="DV")
                    psb_free[pi] = ("DV", dve)
                    if kind == "own":
                        if psf_free[pi] is not None:
                            P.wait("tensor", *psf_free[pi])
                        P.wait("tensor", xs["sem"], xs["ready"])
                        for tile in range(4):
                            pe2 = P.op("tensor", lambda e, _c=tile * D + dc * 128, _p=pi,
                                       _t=tile, _x=xs: e.transpose(
                                out=A["psf"][_p][:, _t * 128:(_t + 1) * 128],
                                in_=_x["buf"][:, _c:_c + 128],
                                identity=id_f[:]), inc="PE")
                        P.wait("vector", "PE", pe2)
                        dve2 = P.op("vector", lambda e, _p=pi, _d=dc * TS, _sg=stgf: e.tensor_copy(
                            out=_sg["buf"][:, _d:_d + TS], in_=A["psf"][_p][:, 0:TS]),
                            inc="DV")
                        psf_free[pi] = ("DV", dve2)
                xs["free"] = [("DV", P.cnt["DV"]), ("PE", P.cnt["PE"])]
                P.wait("gpsimd", "DV")
                dstp = lambda _s=sec: xoT_sw[_s * 128:(_s + 1) * 128, :]
                P.op("gpsimd", lambda e, _sg=stgb: e.dma_start(out=dstp(), in_=_sg["buf"][:]),
                     inc=stgb["sem"], n=16)
                stgb["free"] = (stgb["sem"], P.cnt[stgb["sem"]])
                if kind == "own":
                    for dc in range(NCH):
                        row0 = (sec * 16 + dc) * 128
                        P.op("gpsimd", lambda e, _r=row0, _d=dc, _sg=stgf: e.dma_start(
                            out=xoTf_sw[_r:_r + 128, :],
                            in_=_sg["buf"][:, _d * TS:(_d + 1) * TS]),
                            inc=stgf["sem"], n=16)
                    stgf["free"] = (stgf["sem"], P.cnt[stgf["sem"]])

        if upto >= 2:
            run_phase(xt_alloc, xt_emit)

        with ExitStack() as midst:
            qT = sb(midst, "qT", [128, NCH * TOK], BF, side="right")

            def kvq_alloc(st):
                return {
                    "wbuf": sb(st, "kv_wbuf", [128, NCH * D], BF),
                    "xsec": [sb(st, f"kv_xs{i}", [128, 16 * TS], BF) for i in range(2)],
                    "ostg": [sb(st, f"kv_os{i}", [128, TS], BF) for i in range(4)],
                    "vstg": [sb(st, "kv_vs0", [128, 4 * D], BF)],
                    "pk": [ps(st, f"kv_pk{i}") for i in range(2)],
                }

            def kvq_emit(P, A):
                P.wait("sync", "SC", P.cnt["SC"])
                xsec = Pool(P, A["xsec"], ["qb0", "qb1"])
                ostg = Pool(P, A["ostg"], ["w2", "w3", "w4", "w5"])
                vstg = Pool(P, A["vstg"], ["w6"])
                pk_free = [None, None]
                wsem = ["qa0", "qa1", "qd0"]
                wseq = [0]

                def load_w(wi):
                    sem = wsem[wseq[0] % 3]
                    wseq[0] += 1
                    P.sem(sem)
                    P.active_pool_sems.add(sem)
                    P.wait("sync", "PE", P.cnt["PE"])
                    P.op("sync", lambda e, _wi=wi: e.dma_start(
                        out=A["wbuf"][:], in_=wbf_scr[_wi * 128:(_wi + 1) * 128, :]),
                        inc=sem, n=16)
                    P.wait("tensor", sem, P.cnt[sem])
                    return 0

                def proj_pass(src_plane_fn, nsec, evac_fn):
                    for sec in range(nsec):
                        xs = xsec.load("sync", lambda _s=sec: src_plane_fn(_s))
                        for c in range(NCH):
                            pi = c % 2
                            if pk_free[pi] is not None:
                                P.wait("tensor", *pk_free[pi])
                            P.wait("tensor", xs["sem"], xs["ready"])
                            for dc in range(NCH):
                                P.op("tensor", lambda e, _dc=dc, _c=c, _pi=pi, _x=xs: e.matmul(
                                    out=A["pk"][_pi][:],
                                    lhsT=A["wbuf"][:, _dc * D + _c * 128:_dc * D + _c * 128 + 128],
                                    rhs=_x["buf"][:, _dc * TS:(_dc + 1) * TS],
                                    start=(_dc == 0), stop=(_dc == NCH - 1)),
                                    inc="PE" if dc == NCH - 1 else None)
                            pk_free[pi] = evac_fn(sec, c, pi)
                        xs["free"] = ("PE", P.cnt["PE"])

                load_w(0)

                def k_evac(sec, c, pi):
                    pev = P.cnt["PE"]
                    og = ostg.next()
                    if og["free"] is not None:
                        P.wait("vector", *og["free"])
                    P.wait("vector", "PE", pev)
                    dve = P.op("vector", lambda e, _pi=pi, _o=og: e.tensor_copy(
                        out=_o["buf"][:], in_=A["pk"][_pi][:]), inc="DV")
                    P.wait("gpsimd", "DV", dve)
                    P.op("gpsimd", lambda e, _c=c, _s=sec, _o=og: e.dma_start(
                        out=kc_scr[_c * 128:(_c + 1) * 128, _s * TS:(_s + 1) * TS],
                        in_=_o["buf"][:]), inc=og["sem"], n=16)
                    og["free"] = (og["sem"], P.cnt[og["sem"]])
                    return ("DV", dve)

                proj_pass(lambda s: xoT_sw[s * 128:(s + 1) * 128, :], NL, k_evac)
                for _s in ("w2", "w3", "w4", "w5"):
                    P.wait("gpsimd", _s, P.cnt.get(_s, 0))
                P.op("gpsimd", lambda e: e.collective_compute(
                    "AllGather", mybir.AluOpType.bypass,
                    replica_groups=[[0, 1], [2, 3], [4, 5], [6, 7]],
                    ins=[kc_scr], outs=[kg_scr]), inc="CC", n=1)

                load_w(1)
                w_rdy = P.cnt["DV"]
                for sec in range(NL):
                    xs = xsec.load("sync",
                                   lambda _s=sec: xoT_sw[_s * 128:(_s + 1) * 128, :])
                    vg = vstg.next()
                    if vg["free"] is not None:
                        P.wait("vector", *vg["free"])
                    for stp in range(4):
                        for cs in range(4):
                            pi = (stp * 4 + cs) % 2
                            if pk_free[pi] is not None:
                                P.wait("tensor", *pk_free[pi])
                            P.wait("tensor", xs["sem"], xs["ready"])
                            for dc in range(NCH):
                                P.op("tensor", lambda e, _dc=dc, _st=stp, _cs=cs, _pi=pi, _x=xs: e.matmul(
                                    out=A["pk"][_pi][:],
                                    lhsT=_x["buf"][:, _dc * TS + _st * 128:_dc * TS + _st * 128 + 128],
                                    rhs=A["wbuf"][:, _dc * D + _cs * TS:_dc * D + (_cs + 1) * TS],
                                    start=(_dc == 0), stop=(_dc == NCH - 1)),
                                    inc="PE" if dc == NCH - 1 else None)
                            pev = P.cnt["PE"]
                            P.wait("vector", "PE", pev)
                            dve = P.op("vector", lambda e, _pi=pi, _st=stp, _cs=cs, _v=vg: e.tensor_copy(
                                out=_v["buf"][:, _st * D + _cs * TS:_st * D + (_cs + 1) * TS],
                                in_=A["pk"][_pi][:]), inc="DV")
                            pk_free[pi] = ("DV", dve)
                    P.wait("gpsimd", "DV")
                    for h in range(H):
                        base = vg["buf"][:, h * 128:h * 128 + 128]
                        P.op("gpsimd", lambda e, _h=h, _s=sec, _b=base: e.dma_start(
                            out=vc_scr[_h * 128:(_h + 1) * 128, _s * TS:(_s + 1) * TS],
                            in_=mid3(_b, D, 4)),
                            inc=vg["sem"], n=16)
                    vg["free"] = (vg["sem"], P.cnt[vg["sem"]])
                    xs["free"] = ("PE", P.cnt["PE"])
                P.wait("gpsimd", "w6", P.cnt.get("w6", 0))
                P.op("gpsimd", lambda e: e.collective_compute(
                    "AllGather", mybir.AluOpType.bypass,
                    replica_groups=[[0, 1], [2, 3], [4, 5], [6, 7]],
                    ins=[vc_scr], outs=[vg_scr]), inc="CC", n=1)

                load_w(2)

                def q_evac(sec, c, pi):
                    pev = P.cnt["PE"]
                    P.wait("vector", "PE", pev)
                    dve = P.op("vector", lambda e, _pi=pi, _c=c, _s=sec: e.tensor_copy(
                        out=qT[:, _c * TOK + _s * TS:_c * TOK + (_s + 1) * TS],
                        in_=A["pk"][_pi][:]), inc="DV")
                    return ("DV", dve)

                proj_pass(lambda s: xoT_sw[s * 128:(s + 1) * 128, :], NL, q_evac)

            if upto >= 3:
                run_phase(kvq_alloc, kvq_emit)

            def at_alloc(st):
                return {
                    "kh": [sb(st, f"at_k{i}", [128, T], BF) for i in range(2)],
                    "vh": [sb(st, f"at_v{i}", [128, T], BF) for i in range(2)],
                    "att": [sb(st, f"at_a{i}", [128, TS], BF) for i in range(4)],
                    "mtmp": [sb(st, f"at_m{i}", [128, TS], F32) for i in range(2)],
                    "cs": sb(st, "at_cs", [128, 32], F32),
                    "suff": sb(st, "at_sf", [128, NL], F32),
                    "zf": sb(st, "at_zf", [1, TS], F32),
                    "zr": sb(st, "at_zr", [1, TS], F32),
                    "zb": sb(st, "at_zb", [128, TS], F32),
                    "cstg": [sb(st, f"at_cg{i}", [128, TS], BF) for i in range(8)],
                    "psc": [ps(st, f"at_sc{i}") for i in range(4)],
                    "pz": [ps(st, "at_z0")],
                    "pctx": [ps(st, f"at_cx{i}") for i in range(2)],
                    "pcs": ps(st, "at_pcs"),
                    "mask": sb(st, "mask_sb", [128, NL * 8 * TS], BF),
                }

            def at_emit(P, A):
                P.wait("sync", "SC", P.cnt["SC"])
                kh_p = Pool(P, A["kh"], ["qa0", "qa1"])
                vh_p = Pool(P, A["vh"], ["qb0", "qb1"])
                cstg = Pool(P, A["cstg"],
                            ["w0", "w1", "w2", "w3", "w4", "w5", "w6", "w7"])
                for _s in ("w0", "w1"):
                    P.wait("gpsimd", _s, P.cnt.get(_s, 0))
                P.op("gpsimd", lambda e: e.collective_compute(
                    "AllGather", mybir.AluOpType.bypass,
                    replica_groups=[list(range(8))],
                    ins=[contrib_w], outs=[gath_w]), inc="CC", n=1)
                psc_free = [None, None, None, None]
                pz_free = [None]
                pctx_free = [None, None]
                pcs_free = [None]
                mtmp_free = [None, None]
                mask_sb = A["mask"]
                P.op("sync", lambda e: e.dma_start(out=mask_sb[:], in_=c_mask),
                     inc="IN", n=16)
                mask_rdy = P.cnt["IN"]
                P.wait("sync", "IN", mask_rdy)
                P.wait("vector", "IN", mask_rdy)
                for h in range(H):
                    kh = kh_p.next()
                    if kh["free"] is not None:
                        P.waitl("sync", [kh["free"]])
                    P.wait("sync", "CC", 1)
                    for E in range(8):
                        P.op("sync", lambda e, _h=h, _E=E, _k=kh: e.dma_start(
                            out=_k["buf"][:, _E * TS:(_E + 1) * TS],
                            in_=kg_scr[(_E % 2) * D + _h * 128:(_E % 2) * D + _h * 128 + 128,
                                       (_E // 2) * TS:(_E // 2 + 1) * TS]),
                            inc=kh["sem"], n=16)
                    kh["ready"] = P.cnt[kh["sem"]]
                    vh = vh_p.next()
                    if vh["free"] is not None:
                        P.waitl("sync", [vh["free"]])
                    P.wait("sync", "CC", 2)
                    for E in range(8):
                        P.op("sync", lambda e, _h=h, _E=E, _v=vh: e.dma_start(
                            out=_v["buf"][:, _E * 4 * 128:(_E + 1) * 4 * 128],
                            in_=vg_scr[(_E % 2) * D + _h * 128:(_E % 2) * D + _h * 128 + 128,
                                       (_E // 2) * TS:(_E // 2 + 1) * TS]),
                            inc=vh["sem"], n=16)
                    vh["ready"] = P.cnt[vh["sem"]]
                    if pcs_free[0] is not None:
                        P.wait("tensor", *pcs_free[0])
                    P.wait("tensor", vh["sem"], vh["ready"])
                    P.wait("tensor", kh["sem"], kh["ready"])
                    for j in range(32):
                        P.op("tensor", lambda e, _j=j, _v=vh: e.matmul(
                            out=A["pcs"][:, _j:_j + 1],
                            lhsT=_v["buf"][:, _j * 128:(_j + 1) * 128],
                            rhs=ones_bf[:, 0:1], start=True, stop=True),
                            inc="PE" if j == 31 else None)
                    pev = P.cnt["PE"]
                    P.wait("vector", "PE", pev)
                    dve = P.op("vector", lambda e: e.tensor_copy(
                        out=A["cs"][:], in_=A["pcs"][:, 0:32]), inc="DV")
                    pcs_free[0] = ("DV", dve)
                    for l in range(3):
                        P.op("vector", lambda e, _l=l: e.tensor_reduce(
                            out=A["suff"][:, _l:_l + 1], in_=A["cs"][:, 8 * (_l + 1):32],
                            axis=AX.X, op=Op.add))
                    P.op("vector", lambda e: e.memset(A["suff"][:, 3:4], 0.0), inc="DV")
                    suff_rdy = P.cnt["DV"]
                    jobs = [(l, j) for l in range(NL) for j in range(8 * l + 8)]

                    if True:
                        def emit_score(gi, _h=h, _kh=kh):
                            _l, j = jobs[gi]
                            si = gi % 4
                            rel = j - 8 * _l
                            if psc_free[si] is not None:
                                P.wait("tensor", *psc_free[si])
                            pesc = P.op("tensor", lambda e, _j=j, _si=si: e.matmul(
                                out=A["psc"][_si][:],
                                lhsT=_kh["buf"][:, _j * 128:(_j + 1) * 128],
                                rhs=qT[:, _h * TOK + _l * TS:_h * TOK + (_l + 1) * TS],
                                start=True, stop=True), inc="PE")
                            att = A["att"][gi % 4]
                            if rel >= 0:
                                mi = j % 2
                                P.wait("vector", "PE", pesc)
                                if mtmp_free[mi] is not None:
                                    P.wait("vector", *mtmp_free[mi])
                                dvm = P.op("vector", lambda e, _si=si, _mi=mi, _rel=rel: e.tensor_tensor(
                                    out=A["mtmp"][_mi][:], in0=A["psc"][_si][:],
                                    in1=mask_sb[:, (_l * 8 + _rel) * TS:(_l * 8 + _rel + 1) * TS],
                                    op=Op.mult), inc="DV")
                                psc_free[si] = ("DV", dvm)
                                P.wait("scalar", "DV", dvm)
                                ac = P.op("scalar", lambda e, _mi=mi, _a=att: e.activation(
                                    out=_a[:], in_=A["mtmp"][_mi][:], func=AF.Exp,
                                    scale=INV64), inc="AC")
                                mtmp_free[mi] = ("AC", ac)
                            else:
                                P.wait("scalar", "PE", pesc)
                                ac = P.op("scalar", lambda e, _si=si, _a=att: e.activation(
                                    out=_a[:], in_=A["psc"][_si][:], func=AF.Exp,
                                    scale=INV64), inc="AC")
                                psc_free[si] = ("AC", ac)
                            return ac, att

                        pend = {}
                        LOOK = 4
                        for gi in range(min(LOOK, len(jobs))):
                            pend[gi] = emit_score(gi)
                        for gi, (l, j) in enumerate(jobs):
                            nblk = 8 * l + 8
                            zi, ci = 0, l % 2
                            ac, att = pend.pop(gi)
                            if j == 0 and pz_free[zi] is not None:
                                P.wait("tensor", *pz_free[zi])
                            if j == 0 and pctx_free[ci] is not None:
                                P.wait("tensor", *pctx_free[ci])
                            P.wait("tensor", "AC", ac)
                            P.op("tensor", lambda e, _zi=zi, _a=att, _j=j, _n=nblk: e.matmul(
                                out=A["pz"][_zi][0:1, :], lhsT=ones_bf[:, 0:1], rhs=_a[:],
                                start=(_j == 0), stop=(_j == _n - 1)))
                            P.op("tensor", lambda e, _ci=ci, _a=att, _j=j, _n=nblk, _v=vh: e.matmul(
                                out=A["pctx"][_ci][:],
                                lhsT=_v["buf"][:, _j * 128:(_j + 1) * 128], rhs=_a[:],
                                start=(_j == 0), stop=(_j == _n - 1)),
                                inc="PE" if j == nblk - 1 else None)
                            if gi + LOOK < len(jobs):
                                pend[gi + LOOK] = emit_score(gi + LOOK)
                            if j != nblk - 1:
                                continue
                            pe_pv = P.cnt["PE"]
                            count = float(T - nblk * 128)
                            P.wait("vector", "PE", pe_pv)
                            P.op("vector", lambda e, _zi=zi, _c=count: e.tensor_scalar(
                                out=A["zf"][:], in0=A["pz"][_zi][0:1, :], scalar1=_c,
                                scalar2=0.0, op0=Op.add, op1=Op.add))
                            dvz = P.op("vector", lambda e: e.reciprocal(
                                out=A["zr"][:], in_=A["zf"][:]), inc="DV")
                            pz_free[zi] = ("DV", dvz)
                            if pcs_free[0] is not None:
                                P.wait("tensor", *pcs_free[0])
                            P.wait("tensor", "DV", dvz)
                            pezb = P.op("tensor", lambda e: e.matmul(
                                out=A["pcs"][:], lhsT=ones_f[:], rhs=A["zr"][:],
                                start=True, stop=True), inc="PE")
                            P.wait("vector", "PE", pezb)
                            dvzb = P.op("vector", lambda e: e.tensor_copy(
                                out=A["zb"][:], in_=A["pcs"][:]), inc="DV")
                            pcs_free[0] = ("DV", dvzb)
                            cg = cstg.next()
                            if cg["free"] is not None:
                                P.wait("vector", *cg["free"])
                            P.wait("vector", "DV", suff_rdy)
                            dvc = P.op("vector", lambda e, _ci=ci, _l=l, _cg=cg: e.scalar_tensor_tensor(
                                out=_cg["buf"][:], in0=A["pctx"][_ci][:],
                                scalar=A["suff"][:, _l:_l + 1], in1=A["zb"][:],
                                op0=Op.add, op1=Op.mult), inc="DV")
                            pctx_free[ci] = ("DV", dvc)
                            P.wait("gpsimd", "DV", dvc)
                            P.op("gpsimd", lambda e, _l=l, _h=h, _cg=cg: e.dma_start(
                                out=ctx_sw[_l * 128:(_l + 1) * 128, _h * TS:(_h + 1) * TS],
                                in_=_cg["buf"][:]), inc=cg["sem"], n=16)
                            cg["free"] = (cg["sem"], P.cnt[cg["sem"]])
                    kh["free"] = ("PE", P.cnt["PE"])
                    vh["free"] = ("PE", P.cnt["PE"])

            if upto >= 4:
                run_phase(at_alloc, at_emit)

        def wo_alloc(st):
            return {
                "wbuf": sb(st, "wo_wbuf", [128, NCH * D], BF),
                "ctx": [sb(st, f"wo_cx{i}", [128, H * TS], BF) for i in range(2)],
                "xo": [sb(st, f"wo_xo{i}", [128, TS], F32) for i in range(4)],
                "r": sb(st, "wo_r", [128, NCH * TS], F32),
                "rb": [sb(st, f"wo_rb{i}", [128, TS], BF) for i in range(2)],
                "sq": [sb(st, f"wo_sq{i}", [128, TS], BF) for i in range(2)],
                "stat": sb(st, "wo_st", [1, 6 * TS], F32),
                "mb": sb(st, "wo_mb", [128, TS], F32),
                "rstdb": sb(st, "wo_rs", [128, TS], F32),
                "x1f": [sb(st, f"wo_x1f{i}", [128, TS], F32) for i in range(2)],
                "x1b": [sb(st, f"wo_x1b{i}", [128, TS], BF) for i in range(2)],
                "pao": [ps(st, f"wo_pa{i}") for i in range(3)],
                "ps1": ps(st, "wo_ps1"),
                "ps2": ps(st, "wo_ps2"),
                "pb1": ps(st, "wo_pb1"),
                "pb2": ps(st, "wo_pb2"),
            }

        def ln_tail(P, A, eps, state):
            st_ = A["stat"]
            P.op("vector", lambda e: e.tensor_scalar(
                out=st_[:, 0:TS], in0=A["ps1"][0:1, :], scalar1=1.0 / D, scalar2=0.0, op0=Op.mult, op1=Op.add))
            P.op("vector", lambda e: e.tensor_tensor(
                out=st_[:, 2 * TS:3 * TS], in0=st_[:, 0:TS], in1=st_[:, 0:TS],
                op=Op.mult))
            P.op("vector", lambda e: e.scalar_tensor_tensor(
                out=st_[:, 3 * TS:4 * TS], in0=A["ps2"][0:1, :], scalar=1.0 / D,
                in1=st_[:, 2 * TS:3 * TS], op0=Op.mult, op1=Op.subtract))
            dvv = P.op("vector", lambda e, _e=eps: e.tensor_scalar(
                out=st_[:, 4 * TS:5 * TS], in0=st_[:, 3 * TS:4 * TS], scalar1=_e,
                scalar2=0.0, op0=Op.add, op1=Op.add), inc="DV")
            state["ps_free"] = ("DV", dvv)
            P.wait("scalar", "DV", dvv)
            P.op("scalar", lambda e: e.activation(
                out=st_[:, 5 * TS:6 * TS], in_=st_[:, 4 * TS:5 * TS], func=AF.Ln))
            ac = P.op("scalar", lambda e: e.activation(
                out=st_[:, TS:2 * TS], in_=st_[:, 5 * TS:6 * TS], func=AF.Exp,
                scale=-0.5), inc="AC")
            if state.get("pb_free") is not None:
                P.wait("tensor", *state["pb_free"])
            P.wait("tensor", "AC", ac)
            P.op("tensor", lambda e: e.matmul(
                out=A["pb1"][:], lhsT=ones_f[:], rhs=st_[:, 0:TS],
                start=True, stop=True))
            peb = P.op("tensor", lambda e: e.matmul(
                out=A["pb2"][:], lhsT=ones_f[:], rhs=st_[:, TS:2 * TS],
                start=True, stop=True), inc="PE")
            P.wait("vector", "PE", peb)
            P.op("vector", lambda e: e.tensor_copy(out=A["mb"][:], in_=A["pb1"][:]))
            dvb = P.op("vector", lambda e: e.tensor_copy(
                out=A["rstdb"][:], in_=A["pb2"][:]), inc="DV")
            state["pb_free"] = ("DV", dvb)
            return dvb

        def wo_emit(P, A):
            P.wait("sync", "SC", P.cnt["SC"])
            ctx_p = Pool(P, A["ctx"], ["qb0", "qb1"])
            xo_p = Pool(P, A["xo"], ["qc0", "qc1", "qc2", "qc3"])
            x1f_p = Pool(P, A["x1f"], ["w0", "w1"])
            x1b_p = Pool(P, A["x1b"], ["w2", "w3"])
            pao_free = [None, None, None]
            rb_free = [None, None]
            state = {"ps_free": None, "pb_free": None}
            pend_st = []

            def flush_st(limit):
                while len(pend_st) > limit:
                    acs, ri, first, last = pend_st.pop(0)
                    if first and state["ps_free"] is not None:
                        P.wait("tensor", *state["ps_free"])
                    P.wait("tensor", "AC", acs)
                    P.op("tensor", lambda e, _ri=ri, _f=first, _la=last: e.matmul(
                        out=A["ps1"][0:1, :], lhsT=ones_bf[:, 0:1], rhs=A["rb"][_ri][:],
                        start=_f, stop=_la))
                    pst = P.op("tensor", lambda e, _ri=ri, _f=first, _la=last: e.matmul(
                        out=A["ps2"][0:1, :], lhsT=ones_bf[:, 0:1], rhs=A["sq"][_ri][:],
                        start=_f, stop=_la), inc="PE")
                    rb_free[ri] = ("PE", pst)
            P.sem("qa0")
            P.active_pool_sems.add("qa0")
            P.op("sync", lambda e: e.dma_start(
                out=A["wbuf"][:], in_=wbf_scr[3 * 128:4 * 128, :]), inc="qa0", n=16)
            w_rdy_sem = ("qa0", P.cnt["qa0"])
            for l in range(NL):
                cx = ctx_p.load("sync",
                                lambda _l=l: ctx_sw[_l * 128:(_l + 1) * 128, :])
                for m in range(NCH):
                    pi = m % 3
                    flush_st(1)
                    if pao_free[pi] is not None:
                        P.wait("tensor", *pao_free[pi])
                    P.wait("tensor", cx["sem"], cx["ready"])
                    P.wait("tensor", *w_rdy_sem)
                    for c in range(NCH):
                        P.op("tensor", lambda e, _c=c, _m=m, _pi=pi, _x=cx: e.matmul(
                            out=A["pao"][_pi][:],
                            lhsT=A["wbuf"][:, _c * D + _m * 128:_c * D + _m * 128 + 128],
                            rhs=_x["buf"][:, _c * TS:(_c + 1) * TS],
                            start=(_c == 0), stop=(_c == NCH - 1)),
                            inc="PE" if c == NCH - 1 else None)
                    pev = P.cnt["PE"]
                    xo = xo_p.load("sync",
                                   lambda _l=l, _m=m: xoTf_sw[
                                       (_l * 16 + _m) * 128:(_l * 16 + _m) * 128 + 128, :])
                    P.wait("vector", "PE", pev)
                    P.wait("vector", xo["sem"], xo["ready"])
                    dvr = P.op("vector", lambda e, _pi=pi, _m=m, _xo=xo: e.tensor_tensor(
                        out=A["r"][:, _m * TS:(_m + 1) * TS], in0=A["pao"][_pi][:],
                        in1=_xo["buf"][:], op=Op.add), inc="DV")
                    pao_free[pi] = ("DV", dvr)
                    xo["free"] = ("DV", dvr)
                    ri = m % 2
                    if rb_free[ri] is not None:
                        P.wait("vector", *rb_free[ri])
                    dvrb = P.op("vector", lambda e, _m=m, _ri=ri: e.tensor_copy(
                        out=A["rb"][_ri][:], in_=A["r"][:, _m * TS:(_m + 1) * TS]),
                        inc="DV")
                    P.wait("scalar", "DV", dvrb)
                    acs = P.op("scalar", lambda e, _ri=ri: e.activation(
                        out=A["sq"][_ri][:], in_=A["rb"][_ri][:], func=AF.Square),
                        inc="AC")
                    pend_st.append((acs, ri, m == 0, m == NCH - 1))
                flush_st(0)
                P.wait("vector", "PE", P.cnt["PE"])
                bc_rdy = ln_tail(P, A, EPS_LN, state)
                for m in range(NCH):
                    xf = x1f_p.next()
                    if xf["free"] is not None:
                        P.wait("vector", *xf["free"])
                    xb = x1b_p.next()
                    if xb["free"] is not None:
                        P.wait("vector", *xb["free"])
                    P.wait("vector", "DV", bc_rdy)
                    P.op("vector", lambda e, _m=m, _xf=xf: e.tensor_tensor(
                        out=_xf["buf"][:], in0=A["r"][:, _m * TS:(_m + 1) * TS],
                        in1=A["mb"][:], op=Op.subtract))
                    P.op("vector", lambda e, _xf=xf: e.tensor_tensor(
                        out=_xf["buf"][:], in0=_xf["buf"][:], in1=A["rstdb"][:],
                        op=Op.mult))
                    dvx = P.op("vector", lambda e, _xf=xf, _xb=xb: e.tensor_copy(
                        out=_xb["buf"][:], in_=_xf["buf"][:]), inc="DV")
                    P.wait("gpsimd", "DV", dvx)
                    P.op("gpsimd", lambda e, _l=l, _m=m, _xf=xf: e.dma_start(
                        out=x1f_sw[(_l * 16 + _m) * 128:(_l * 16 + _m) * 128 + 128, :],
                        in_=_xf["buf"][:]), inc=xf["sem"], n=16)
                    xf["free"] = (xf["sem"], P.cnt[xf["sem"]])
                    P.op("gpsimd", lambda e, _l=l, _m=m, _xb=xb: e.dma_start(
                        out=x1b_sw[_l * 128:(_l + 1) * 128, _m * TS:(_m + 1) * TS],
                        in_=_xb["buf"][:]), inc=xb["sem"], n=16)
                    xb["free"] = (xb["sem"], P.cnt[xb["sem"]])
                cx["free"] = ("PE", P.cnt["PE"])

        if upto >= 5:
            run_phase(wo_alloc, wo_emit)

        for l in range(NL):
            if upto < 6:
                break
            with ExitStack() as sl:
                h_strip = sb(sl, f"f_h{l}", [128, 64 * TS], BF)
                invb = sb(sl, f"f_inv{l}", [128, TS], F32)

                def fp_alloc(st, _l=l):
                    return {
                        "x1s": [sb(st, f"f_x1_{_l}", [128, 16 * TS], BF)],
                        "wg": [sb(st, f"f_wg{i}_{_l}", [128, 16 * TS], BF) for i in range(2)],
                        "wu": [sb(st, f"f_wu{i}_{_l}", [128, 16 * TS], BF) for i in range(2)],
                        "sil": [sb(st, f"f_sil{i}_{_l}", [128, TS], F32) for i in range(2)],
                        "sq": [sb(st, f"f_sq{i}_{_l}", [128, TS], BF) for i in range(4)],
                        "s2": sb(st, f"f_s2_{_l}", [1, 3 * TS], F32),
                        "pg": [ps(st, f"f_pg{i}_{_l}") for i in range(2)],
                        "pu": [ps(st, f"f_pu{i}_{_l}") for i in range(2)],
                        "phs": ps(st, f"f_phs_{_l}"),
                        "pib": ps(st, f"f_pib_{_l}"),
                    }

                def fp_emit(P, A, _l=l):
                    P.wait("sync", "SC", P.cnt["SC"])
                    x1_p = Pool(P, A["x1s"], ["qa0"])
                    wg_p = Pool(P, A["wg"], ["qb0", "qb1"])
                    wu_p = Pool(P, A["wu"], ["qc0", "qc1"])
                    pg_free = [None, None]
                    pu_free = [None, None]
                    sil_free = [None, None]
                    sq_free = [None, None, None, None]
                    pend_sq = []

                    def flush_sq(limit):
                        while len(pend_sq) > limit:
                            acq, qi, first, last = pend_sq.pop(0)
                            P.wait("tensor", "AC", acq)
                            pq = P.op("tensor", lambda e, _qi=qi, _f=first, _la=last: e.matmul(
                                out=A["phs"][0:1, :], lhsT=ones_bf[:, 0:1],
                                rhs=A["sq"][_qi][:],
                                start=_f, stop=_la), inc="PE")
                            sq_free[qi] = ("PE", pq)
                    P.wait("sync", "CC", 3)
                    x1 = x1_p.load("sync",
                                   lambda: x1b_sw[_l * 128:(_l + 1) * 128, :])
                    for g in range(16):
                        wg = wg_p.load("sync",
                                       lambda _g=g: gath_w[wgu_row(_g):wgu_row(_g) + 128, :])
                        wu = wu_p.load("sync",
                                       lambda _g=g: gath_w[wgu_row(16 + _g):wgu_row(16 + _g) + 128, :])
                        for k in range(4):
                            pi = k % 2
                            flush_sq(2)
                            if pg_free[pi] is not None:
                                P.wait("tensor", *pg_free[pi])
                            if pu_free[pi] is not None:
                                P.wait("tensor", *pu_free[pi])
                            P.wait("tensor", x1["sem"], x1["ready"])
                            P.wait("tensor", wg["sem"], wg["ready"])
                            P.wait("tensor", wu["sem"], wu["ready"])
                            for dc in range(NCH):
                                P.op("tensor", lambda e, _dc=dc, _k=k, _pi=pi, _w=wg, _x=x1: e.matmul(
                                    out=A["pg"][_pi][:],
                                    lhsT=_w["buf"][:, _dc * TS + _k * 128:_dc * TS + _k * 128 + 128],
                                    rhs=_x["buf"][:, _dc * TS:(_dc + 1) * TS],
                                    start=(_dc == 0), stop=(_dc == NCH - 1)),
                                    inc="PE" if dc == NCH - 1 else None)
                            peg = P.cnt["PE"]
                            for dc in range(NCH):
                                P.op("tensor", lambda e, _dc=dc, _k=k, _pi=pi, _w=wu, _x=x1: e.matmul(
                                    out=A["pu"][_pi][:],
                                    lhsT=_w["buf"][:, _dc * TS + _k * 128:_dc * TS + _k * 128 + 128],
                                    rhs=_x["buf"][:, _dc * TS:(_dc + 1) * TS],
                                    start=(_dc == 0), stop=(_dc == NCH - 1)),
                                    inc="PE" if dc == NCH - 1 else None)
                            peu = P.cnt["PE"]
                            P.wait("scalar", "PE", peg)
                            if sil_free[pi] is not None:
                                P.wait("scalar", *sil_free[pi])
                            acg = P.op("scalar", lambda e, _pi=pi: e.activation(
                                out=A["sil"][_pi][:], in_=A["pg"][_pi][:], func=AF.Silu),
                                inc="AC")
                            pg_free[pi] = ("AC", acg)
                            P.wait("vector", "AC", acg)
                            P.wait("vector", "PE", peu)
                            hcol = (g * 4 + k) * TS
                            dvh = P.op("vector", lambda e, _pi=pi, _hc=hcol: e.tensor_tensor(
                                out=h_strip[:, _hc:_hc + TS], in0=A["sil"][_pi][:],
                                in1=A["pu"][_pi][:], op=Op.mult), inc="DV")
                            pu_free[pi] = ("DV", dvh)
                            sil_free[pi] = ("DV", dvh)
                            P.wait("scalar", "DV", dvh)
                            qi = (g * 4 + k) % 4
                            if sq_free[qi] is not None:
                                P.wait("scalar", *sq_free[qi])
                            acq = P.op("scalar", lambda e, _qi=qi, _hc=hcol: e.activation(
                                out=A["sq"][_qi][:], in_=h_strip[:, _hc:_hc + TS],
                                func=AF.Square), inc="AC")
                            pend_sq.append((acq, qi, g == 0 and k == 0,
                                            g == 15 and k == 3))
                        wg["free"] = ("PE", P.cnt["PE"])
                        wu["free"] = ("PE", P.cnt["PE"])
                    flush_sq(0)
                    x1["free"] = ("PE", P.cnt["PE"])
                    P.wait("vector", "PE", P.cnt["PE"])
                    dvs = P.op("vector", lambda e: e.tensor_scalar(
                        out=A["s2"][:, 0:TS], in0=A["phs"][0:1, :], scalar1=1.0 / HF,
                        scalar2=EPS_RMS, op0=Op.mult, op1=Op.add), inc="DV")
                    P.wait("scalar", "DV", dvs)
                    P.op("scalar", lambda e: e.activation(
                        out=A["s2"][:, TS:2 * TS], in_=A["s2"][:, 0:TS], func=AF.Ln))
                    aci = P.op("scalar", lambda e: e.activation(
                        out=A["s2"][:, 2 * TS:3 * TS], in_=A["s2"][:, TS:2 * TS],
                        func=AF.Exp, scale=-0.5), inc="AC")
                    P.wait("tensor", "AC", aci)
                    pei = P.op("tensor", lambda e: e.matmul(
                        out=A["pib"][:], lhsT=ones_f[:], rhs=A["s2"][:, 2 * TS:3 * TS],
                        start=True, stop=True), inc="PE")
                    P.wait("vector", "PE", pei)
                    P.op("vector", lambda e: e.tensor_copy(out=invb[:], in_=A["pib"][:]),
                         inc="DV")

                run_phase(fp_alloc, fp_emit)
                h_rdy = P.cnt["DV"]

                def pj_alloc(st, _l=l):
                    return {
                        "wpm": [sb(st, f"f_wp{i}_{_l}", [128, 64 * 128], BF) for i in range(2)],
                        "x1f": [sb(st, f"f_x1f{i}_{_l}", [128, TS], F32) for i in range(2)],
                        "r2": sb(st, f"f_r2_{_l}", [128, NCH * TS], F32),
                        "rb": [sb(st, f"f_rb{i}_{_l}", [128, TS], BF) for i in range(2)],
                        "sq": [sb(st, f"f_sq2{i}_{_l}", [128, TS], BF) for i in range(2)],
                        "stat": sb(st, f"f_st_{_l}", [1, 6 * TS], F32),
                        "mb": sb(st, f"f_mb_{_l}", [128, TS], F32),
                        "rstdb": sb(st, f"f_rs_{_l}", [128, TS], F32),
                        "onv": [sb(st, f"f_on{i}_{_l}", [128, TS], F32) for i in range(2)],
                        "ostg": [sb(st, f"f_og{i}_{_l}", [128, D], F32) for i in range(4)],
                        "pp": [ps(st, f"f_pp{i}_{_l}") for i in range(2)],
                        "ps1": ps(st, f"f_ps1_{_l}"),
                        "ps2": ps(st, f"f_ps2_{_l}"),
                        "pb1": ps(st, f"f_pb1_{_l}"),
                        "pb2": ps(st, f"f_pb2_{_l}"),
                        "pot": [ps(st, f"f_pt{i}_{_l}") for i in range(2)],
                    }

                def pj_emit(P, A, _l=l, _h_rdy=h_rdy):
                    P.wait("sync", "SC", P.cnt["SC"])
                    wpm_p = Pool(P, A["wpm"], ["qa0", "qa1"])
                    x1f_p = Pool(P, A["x1f"], ["qb0", "qb1"])
                    ostg = Pool(P, A["ostg"], ["w0", "w1", "w2", "w3"])
                    pp_free = [None, None]
                    rb_free = [None, None]
                    onv_free = [None, None]
                    pot_free = [None, None]
                    state = {"ps_free": None, "pb_free": None}
                    pend_st = []

                    def flush_st(limit):
                        while len(pend_st) > limit:
                            acs, ri, first, last = pend_st.pop(0)
                            if first and state["ps_free"] is not None:
                                P.wait("tensor", *state["ps_free"])
                            P.wait("tensor", "AC", acs)
                            P.op("tensor", lambda e, _ri=ri, _f=first, _la=last: e.matmul(
                                out=A["ps1"][0:1, :], lhsT=ones_bf[:, 0:1],
                                rhs=A["rb"][_ri][:], start=_f, stop=_la))
                            pst = P.op("tensor", lambda e, _ri=ri, _f=first, _la=last: e.matmul(
                                out=A["ps2"][0:1, :], lhsT=ones_bf[:, 0:1],
                                rhs=A["sq"][_ri][:], start=_f, stop=_la), inc="PE")
                            rb_free[ri] = ("PE", pst)
                    for m in range(NCH):
                        P.wait("sync", "CC", 3)
                        wp = wpm_p.load("sync",
                                        lambda _m=m: gath_w[wp_row(_m):wp_row(_m) + 128, :])
                        pi = m % 2
                        flush_st(1)
                        if pp_free[pi] is not None:
                            P.wait("tensor", *pp_free[pi])
                        P.wait("tensor", wp["sem"], wp["ready"])
                        P.wait("tensor", "DV", _h_rdy)
                        for hc in range(64):
                            P.op("tensor", lambda e, _hc=hc, _pi=pi, _w=wp: e.matmul(
                                out=A["pp"][_pi][:],
                                lhsT=_w["buf"][:, _hc * 128:(_hc + 1) * 128],
                                rhs=h_strip[:, _hc * TS:(_hc + 1) * TS],
                                start=(_hc == 0), stop=(_hc == 63)),
                                inc="PE" if hc == 63 else None)
                        pev = P.cnt["PE"]
                        wp["free"] = ("PE", pev)
                        xf = x1f_p.load("sync",
                                        lambda _m=m: x1f_sw[
                                            (_l * 16 + _m) * 128:(_l * 16 + _m) * 128 + 128, :])
                        P.wait("vector", "PE", pev)
                        P.wait("vector", xf["sem"], xf["ready"])
                        P.op("vector", lambda e, _pi=pi, _m=m: e.tensor_tensor(
                            out=A["r2"][:, _m * TS:(_m + 1) * TS], in0=A["pp"][_pi][:],
                            in1=invb[:], op=Op.mult))
                        dvr = P.op("vector", lambda e, _m=m, _xf=xf: e.tensor_tensor(
                            out=A["r2"][:, _m * TS:(_m + 1) * TS],
                            in0=A["r2"][:, _m * TS:(_m + 1) * TS],
                            in1=_xf["buf"][:], op=Op.add), inc="DV")
                        pp_free[pi] = ("DV", dvr)
                        xf["free"] = ("DV", dvr)
                        ri = m % 2
                        if rb_free[ri] is not None:
                            P.wait("vector", *rb_free[ri])
                        dvrb = P.op("vector", lambda e, _m=m, _ri=ri: e.tensor_copy(
                            out=A["rb"][_ri][:], in_=A["r2"][:, _m * TS:(_m + 1) * TS]),
                            inc="DV")
                        P.wait("scalar", "DV", dvrb)
                        acs = P.op("scalar", lambda e, _ri=ri: e.activation(
                            out=A["sq"][_ri][:], in_=A["rb"][_ri][:], func=AF.Square),
                            inc="AC")
                        pend_st.append((acs, ri, m == 0, m == NCH - 1))
                    flush_st(0)
                    P.wait("vector", "PE", P.cnt["PE"])
                    bc_rdy = ln_tail(P, A, EPS_LN, state)
                    ogs = [ostg.next() for _ in range(4)]
                    for og in ogs:
                        if og["free"] is not None:
                            P.wait("vector", *og["free"])
                    for m in range(NCH):
                        oi = m % 2
                        P.wait("vector", "DV", bc_rdy)
                        if onv_free[oi] is not None:
                            P.wait("vector", *onv_free[oi])
                        P.op("vector", lambda e, _m=m, _oi=oi: e.tensor_tensor(
                            out=A["onv"][_oi][:], in0=A["r2"][:, _m * TS:(_m + 1) * TS],
                            in1=A["mb"][:], op=Op.subtract))
                        dvo = P.op("vector", lambda e, _oi=oi: e.tensor_tensor(
                            out=A["onv"][_oi][:], in0=A["onv"][_oi][:], in1=A["rstdb"][:],
                            op=Op.mult), inc="DV")
                        for stt in range(4):
                            ti = (m * 4 + stt) % 2
                            if pot_free[ti] is not None:
                                P.wait("tensor", *pot_free[ti])
                            P.wait("tensor", "DV", dvo)
                            pet = P.op("tensor", lambda e, _oi=oi, _st=stt, _ti=ti: e.transpose(
                                out=A["pot"][_ti][:, 0:128],
                                in_=A["onv"][_oi][:, _st * 128:(_st + 1) * 128],
                                identity=id_f[:]), inc="PE")
                            P.wait("vector", "PE", pet)
                            dve = P.op("vector", lambda e, _ti=ti, _st=stt, _m=m: e.tensor_copy(
                                out=ogs[_st]["buf"][:, _m * 128:(_m + 1) * 128],
                                in_=A["pot"][_ti][:, 0:128]), inc="DV")
                            pot_free[ti] = ("DV", dve)
                        onv_free[oi] = ("PE", P.cnt["PE"])
                    P.wait("gpsimd", "DV")
                    for stt in range(4):
                        P.op("gpsimd", lambda e, _st=stt: e.dma_start(
                            out=out[_l * TS + _st * 128:_l * TS + (_st + 1) * 128, :],
                            in_=ogs[_st]["buf"][:]), inc=ogs[stt]["sem"], n=16)
                        ogs[stt]["free"] = (ogs[stt]["sem"], P.cnt[ogs[stt]["sem"]])

                run_phase(pj_alloc, pj_emit)

        def fin_emit(P, A):
            P.wait("gpsimd", "SC", P.cnt["SC"])
            P.wait("gpsimd", "IN", P.cnt["IN"])

        run_phase(lambda st: {}, fin_emit)

    return nc


def make_in_maps(x, Wq, Wk, Wv, Wo, wgu, wp):
    wgu_shards = [np.ascontiguousarray(np.concatenate(
        [wgu[(c + 8 * k) * 512:(c + 8 * k + 1) * 512] for k in range(4)], axis=0))
        for c in range(8)]
    wp_shards = [np.ascontiguousarray(np.concatenate(
        [wp[(c + 8 * k) * 128:(c + 8 * k + 1) * 128] for k in range(2)], axis=0))
        for c in range(8)]
    in_maps = []
    ones_bf = np.ones((128, 128), dtype=NPBF)
    ones_f = np.ones((1, 128), dtype=np.float32)
    id_bf = np.eye(128, dtype=np.float32).astype(NPBF)
    id_f = np.eye(128, dtype=np.float32)
    pp = np.arange(128)[:, None]
    tt = np.arange(TS)[None, :]
    for c in range(8):
        b, half = c // 2, c % 2
        xb = np.ascontiguousarray(x[b])
        x_own = np.ascontiguousarray(
            np.concatenate([xb[(2 * l + half) * TS:(2 * l + half + 1) * TS]
                            for l in range(NL)], axis=0))
        mb = np.zeros((128, NL * 8 * TS), dtype=NPBF)
        for l in range(NL):
            E = 2 * l + half
            for rel in range(8):
                j = 8 * l + rel
                keep = (j * 128 + pp) <= (E * TS + tt)
                mb[:, (l * 8 + rel) * TS:(l * 8 + rel + 1) * TS] = keep.astype(NPBF)
        in_maps.append({
            "x_own": x_own,
            "wq": Wq, "wk": Wk, "wv": Wv, "wo": Wo,
            "wgu_my": wgu_shards[c], "wp_my": wp_shards[c],
            "c_mask": mb,
            "c_ones_bf": ones_bf,
            "c_ones_f": ones_f,
            "c_id_bf": id_bf,
            "c_id_f": id_f,
        })
    return in_maps


def kernel(L0_attn_x, L0_attn_Wq, L0_attn_Wk, L0_attn_Wv, L0_attn_Wo,
           L0_ffn_w_gate_up, L0_ffn_w_proj):
    x = np.ascontiguousarray(L0_attn_x, dtype=np.float32)
    nc = build()
    in_maps = make_in_maps(
        x,
        np.ascontiguousarray(L0_attn_Wq, dtype=np.float32),
        np.ascontiguousarray(L0_attn_Wk, dtype=np.float32),
        np.ascontiguousarray(L0_attn_Wv, dtype=np.float32),
        np.ascontiguousarray(L0_attn_Wo, dtype=np.float32),
        np.ascontiguousarray(L0_ffn_w_gate_up, dtype=np.float32),
        np.ascontiguousarray(L0_ffn_w_proj, dtype=np.float32))
    res = run_bass_kernel_spmd(nc, in_maps, core_ids=list(range(8)))
    outf = np.empty((B, T, D), dtype=np.float32)
    for c in range(8):
        b, half = c // 2, c % 2
        oc = res.results[c]["out"]
        for l in range(NL):
            E = 2 * l + half
            outf[b, E * TS:(E + 1) * TS] = oc[l * TS:(l + 1) * TS]
    return outf
